# version 06
# baseline (speedup 1.0000x reference)
import numpy as np
import ml_dtypes
import contextlib
import concourse.bass as bass
import concourse.mybir as mybir
from concourse.bass_utils import run_bass_kernel_spmd

F32 = mybir.dt.float32
BF16 = mybir.dt.bfloat16
AF = mybir.ActivationFunctionType
ALU = mybir.AluOpType
AX = mybir.AxisListType

P = 128
C = 64
NH = 16
D = 2048
DH = 5632
T = 2048
NT = T // P
NCH = T // C
NCORE = 8
MAGIC = float(3 * 2.0 ** 22)
C0 = float(np.log(np.sqrt(128.0)))
LN_EPS = 1e-5
Q_EPS = 1e-5
W2ROWS = DH // NCORE


class Sem:
    def __init__(self, h):
        self.h = h
        self.n = 0

    def inc(self, instr, k=1):
        instr.then_inc(self.h, k)
        self.n += k
        return self.n


class Guard:

    def __init__(self, depth=2):
        self.hist = [0] * depth

    def tok(self):
        return self.hist[0]

    def push(self, t):
        self.hist = self.hist[1:] + [t]


import os
PHASE_LIMIT = int(os.environ.get("KERN_PHASES", "9"))
P4_NOHQ = bool(int(os.environ.get("KERN_P4_NOHQ", "0")))
P4_NST = int(os.environ.get("KERN_P4_NST", "4"))
P4_NOEV = bool(int(os.environ.get("KERN_P4_NOEV", "0")))
KDBG = bool(int(os.environ.get("KERN_DEBUG", "0")))
P4_EVMODE = int(os.environ.get("KERN_P4_EVMODE", "3"))


class RecInstr:
    __slots__ = ("incs",)

    def __init__(self):
        self.incs = []

    def then_inc(self, sem, k=1):
        self.incs.append((sem, k))
        return self


class RecEng:
    def __init__(self, rec, key):
        self._rec = rec
        self._key = key

    def __getattr__(self, name):
        rec, key = self._rec, self._key

        def call(*a, **kw):
            ri = RecInstr()
            rec.ops[key].append((name, a, kw, ri))
            return ri
        return call


class Recorder:
    KEYS = ("sync", "scalar", "vector", "tensor", "gpsimd")

    def __init__(self, nc):
        self.nc = nc
        self.ops = {k: [] for k in self.KEYS}

    def _section(self, key):
        def deco(f):
            f(RecEng(self, key))
        return deco

    @property
    def sync(self):
        return self._section("sync")

    @property
    def scalar(self):
        return self._section("scalar")

    @property
    def vector(self):
        return self._section("vector")

    @property
    def tensor(self):
        return self._section("tensor")

    @property
    def gpsimd(self):
        return self._section("gpsimd")

    def flush(self):
        pend = {k: v for k, v in self.ops.items() if v}
        if not pend:
            return
        self.ops = {k: [] for k in self.KEYS}
        with self.nc.Block() as blk:
            for key, ops in pend.items():
                def mk(ops):
                    def run(eng):
                        for (name, a, kw, ri) in ops:
                            instr = getattr(eng, name)(*a, **kw)
                            for (sem, k) in ri.incs:
                                instr.then_inc(sem, k)
                    return run
                getattr(blk, key)(mk(ops))


def build_program():
    nc = bass.Bass("TRN2", target_bir_lowering=False, debug=False)

    ap_x = nc.dram_tensor("x", [T, D], F32, kind="ExternalInput").ap()
    ap_wq = nc.dram_tensor("wq", [D, D], F32, kind="ExternalInput").ap()
    ap_wk = nc.dram_tensor("wk", [D, D], F32, kind="ExternalInput").ap()
    ap_wi = nc.dram_tensor("wi", [D, D], F32, kind="ExternalInput").ap()
    ap_wo = nc.dram_tensor("wo", [D, D], F32, kind="ExternalInput").ap()
    ap_w1own = nc.dram_tensor("w1own", [D // NCORE, DH], F32, kind="ExternalInput").ap()
    ap_w3own = nc.dram_tensor("w3own", [D // NCORE, DH], F32, kind="ExternalInput").ap()
    ap_w2own = nc.dram_tensor("w2own", [W2ROWS, D], F32, kind="ExternalInput").ap()
    ap_gw = nc.dram_tensor("gwv", [1, D], F32, kind="ExternalInput").ap()
    ap_cmask = nc.dram_tensor("cmask", [1, 8], F32, kind="ExternalInput").ap()
    ap_tri_i = nc.dram_tensor("ctri_i", [P, P], BF16, kind="ExternalInput").ap()
    ap_tri_x = nc.dram_tensor("ctri_x", [P, P], BF16, kind="ExternalInput").ap()
    ap_ident = nc.dram_tensor("cident", [P, P], BF16, kind="ExternalInput").ap()
    ap_maskp = nc.dram_tensor("cmaskp", [P, 1024], F32, kind="ExternalInput").ap()
    ap_csel = nc.dram_tensor("ccsel", [P, 2], BF16, kind="ExternalInput").ap()
    ap_ones = nc.dram_tensor("cones", [P, 1], BF16, kind="ExternalInput").ap()
    ap_y = nc.dram_tensor("y", [T, D], F32, kind="ExternalOutput").ap()

    d_sph = nc.dram_tensor("d_sph", [T, D], BF16).ap()
    d_spl = nc.dram_tensor("d_spl", [T, D], BF16).ap()
    d_d = nc.dram_tensor("d_d", [T, D], F32).ap()
    d_v = nc.dram_tensor("d_v", [T, D], BF16).ap()
    d_q = nc.dram_tensor("d_q", [T, D], BF16).ap()
    d_xqf = nc.dram_tensor("d_xqf", [T, D], BF16).ap()
    d_qh = nc.dram_tensor("d_qh", [P, NT * D], BF16).ap()
    d_ol = nc.dram_tensor("d_ol", [NCH * P, 1024], BF16).ap()
    d_hq = nc.dram_tensor("d_hq", [T, DH], BF16).ap()
    d_gate = nc.dram_tensor("d_gate", [T, D], BF16).ap()
    d_scl = nc.dram_tensor("d_scl", [1, 64], F32).ap()

    ar_in = nc.dram_tensor("ar_in", [1, 16], F32).ap()
    ar_out = nc.dram_tensor("ar_out", [1, 16], F32).ap()
    wag1_in = nc.dram_tensor("wag1_in", [D // NCORE, DH], BF16).ap()
    wag1_out = nc.dram_tensor("wag1_out", [D, DH], BF16).ap()
    wag3_in = nc.dram_tensor("wag3_in", [D // NCORE, DH], BF16).ap()
    wag3_out = nc.dram_tensor("wag3_out", [D, DH], BF16).ap()
    wag2_in = nc.dram_tensor("wag2_in", [W2ROWS, D], BF16).ap()
    wag2_out = nc.dram_tensor("wag2_out", [DH, D], BF16).ap()
    pub_in = nc.dram_tensor("pub_in", [P, 2064], F32).ap()
    pub_out = nc.dram_tensor("pub_out", [NCORE * P, 2064], F32).ap()
    dbg = {}
    d_wsl = nc.dram_tensor("d_wsl", [4 * P, 16 * 512], BF16).ap()
    d_warb = nc.dram_tensor("d_warb", [P, 24], F32).ap()
    d_stats = nc.dram_tensor("d_stats", [P, 8 * NT], F32).ap()
    dbg2 = {}
    if KDBG:
        dbg2["d_stats"] = nc.dram_tensor("dbg_d_stats", [P, 8 * NT], F32, kind="ExternalOutput").ap()
    if KDBG:
        for nm, shp, dt in [("d_wsl", [4 * P, 16 * 512], BF16),
                            ("ar_out", [1, 16], F32), ("ar_in", [1, 16], F32),
                            ("d_warb", [P, 24], F32), ("d_sph", [T, D], BF16),
                            ("d_d", [T, D], F32), ("d_v", [T, D], BF16), ("d_q", [T, D], BF16),
                            ("d_xqf", [T, D], BF16), ("d_qh", [P, NT * D], BF16),
                            ("d_ol", [NCH * P, 1024], BF16), ("d_hq", [T, DH], BF16),
                            ("d_gate", [T, D], BF16), ("pub_out", [NCORE * P, 2064], F32),
                            ("wag1_out", [D, DH], BF16), ("d_scl", [1, 64], F32)]:
            dbg[nm] = nc.dram_tensor("dbg_" + nm, shp, dt, kind="ExternalOutput").ap()

    RG = [list(range(NCORE))]

    es = contextlib.ExitStack()
    with es:
        sb = lambda name, shape, dt=F32: es.enter_context(nc.sbuf_tensor(name, shape, dt))
        psr = lambda name, shape, dt=F32: es.enter_context(nc.psum_tensor(name, shape, dt))
        mks = lambda name: Sem(es.enter_context(nc.semaphore(name)))
        block = Recorder(nc)

        tri_i = sb("s_tri_i", [P, P], BF16)
        tri_x = sb("s_tri_x", [P, P], BF16)
        ident = sb("s_ident", [P, P], BF16)
        maskp = sb("s_maskp", [P, 1024], F32)
        csel = sb("s_csel", [P, 2], BF16)
        ones1 = sb("s_ones1", [P, 1], BF16)
        gw_fm = sb("s_gw_fm", [P, NH], F32)
        cmsk = sb("s_cmsk", [P, 8], F32)
        sx_all = sb("s_sx_all", [P, NT], F32)
        sxi_all = sb("s_sxi_all", [P, NT], F32)
        wpart = sb("s_wpart", [P, 16], F32)
        wsums = sb("s_wsums", [1, 16], F32)
        wscl = sb("s_wscl", [P, 8], F32)
        rs1_all = sb("s_rs1", [P, NT], F32)
        rs3_all = sb("s_rs3", [P, NT], F32)
        rs2_all = sb("s_rs2", [P, NT], F32)
        st1 = sb("s_st1", [P, 1], F32)
        st2 = sb("s_st2", [P, 1], F32)
        st3 = sb("s_st3", [P, 1], F32)
        st4 = sb("s_st4", [P, 1], F32)
        ones1f = sb("s_ones1f", [P, 1], F32)
        warb = sb("s_warb", [P, 16], F32)
        warb2 = sb("s_warb2", [P, 8], F32)
        stk = sb("s_stk", [P, 1], F32)

        ps_a = psr("ps_a", [P, 512], F32)
        ps_b = psr("ps_b", [P, 512], F32)
        ps_c = psr("ps_c", [P, 512], F32)
        ps_tp = psr("ps_tp", [P, 1024], BF16)
        ps_d2 = psr("ps_d2", [P, 1024], F32)
        ps_e2 = psr("ps_e2", [P, 1024], F32)

        s_ld = mks("m_ld")
        s_x01 = [mks("m_x0"), mks("m_x1")]
        s_wq01 = [mks("m_wq0"), mks("m_wq1")]
        s_wl01 = [mks("m_wl0"), mks("m_wl1"), mks("m_wl2"), mks("m_wl3")]
        s_pst01 = [mks("m_pst0"), mks("m_pst1")]
        s_pso01 = [mks("m_pso0"), mks("m_pso1")]
        s_p2l01 = [mks("m_p2l0"), mks("m_p2l1")]
        s_ost = mks("m_ost")
        s_qhst = mks("m_qhst")
        s_xqf = mks("m_xqf")
        s_xq4 = mks("m_xq4")
        s_w13 = [mks("m_w13a"), mks("m_w13b")]
        s_hqst = mks("m_hqst")
        s_w2l = mks("m_w2l")
        s_hql01 = [mks("m_hql0"), mks("m_hql1")]
        s_gt01 = [mks("m_gt0"), mks("m_gt1")]
        s_p6l01 = [mks("m_p6l0"), mks("m_p6l1")]
        s_wol01 = [mks("m_wol0"), mks("m_wol1")]
        s_uj = mks("m_uj")
        s_y = mks("m_y")
        s_mm = mks("m_mm")
        s_ve = mks("m_ve")
        s_sc = mks("m_sc")
        s_st = mks("m_st")
        s_cc = mks("m_cc")
        s_fr = mks("m_fr")

        def sbx(stk, name, shape, dt=F32):
            return stk.enter_context(nc.sbuf_tensor(name, shape, dt))

        def join():
            tv, ts, tm, tl, tst = s_ve.n, s_sc.n, s_mm.n, s_ld.n, s_st.n

            @block.vector
            def _(v):
                v.wait_ge(s_sc.h, ts)
                v.wait_ge(s_mm.h, tm)
                v.wait_ge(s_st.h, tst)

            @block.scalar
            def _(sc):
                sc.wait_ge(s_ve.h, tv)
                sc.wait_ge(s_mm.h, tm)

            @block.tensor
            def _(t):
                t.wait_ge(s_ve.h, tv)
                t.wait_ge(s_sc.h, ts)
                t.wait_ge(s_ld.h, tl)

            @block.sync
            def _(sync):
                sync.wait_ge(s_ve.h, tv)
                sync.wait_ge(s_sc.h, ts)
                sync.wait_ge(s_mm.h, tm)
                sync.wait_ge(s_st.h, tst)

        def finish():
            @block.sync
            def _(sync):
                sync.wait_ge(s_st.h, s_st.n)
                sync.wait_ge(s_y.h, s_y.n)
            block.flush()

        @block.sync
        def _(sync):
            s_ld.inc(sync.dma_start(out=tri_i[:], in_=ap_tri_i[:, :]), 16)
            s_ld.inc(sync.dma_start(out=tri_x[:], in_=ap_tri_x[:, :]), 16)
            s_ld.inc(sync.dma_start(out=ident[:], in_=ap_ident[:, :]), 16)
            s_ld.inc(sync.dma_start(out=maskp[:], in_=ap_maskp[:, :]), 16)
            s_ld.inc(sync.dma_start(out=csel[:], in_=ap_csel[:, :]), 16)
            s_ld.inc(sync.dma_start(out=ones1[:], in_=ap_ones[:, :]), 16)
            s_ld.inc(sync.dma_start(out=cmsk[:, :], in_=ap_cmask[0:1, 0:8].broadcast_to([P, 8])), 16)
            s_ld.inc(sync.dma_start(
                out=gw_fm[:, :], in_=ap_gw[0:1, :].rearrange("o (h p) -> (o p) h", h=NH),
                allow_slow_non_contiguous=True), 16)
        const_tok = s_ld.n

        esA = es.enter_context(contextlib.ExitStack())
        xn_fm = sbx(esA, "s_xn_fm", [P, NT * D], BF16)
        wsl = [sbx(esA, f"s_wsl{b}", [P, 16 * 512], BF16) for b in range(4)]
        wld = [sbx(esA, f"s_wld{b}", [P, 512], F32) for b in range(4)]
        ev1 = [sbx(esA, f"s_ev1{b}", [P, 512], F32) for b in range(2)]
        ev2 = [sbx(esA, f"s_ev2{b}", [P, 512], F32) for b in range(2)]
        evh = [sbx(esA, f"s_evh{b}", [P, 512], BF16) for b in range(2)]
        evl = [sbx(esA, f"s_evl{b}", [P, 512], BF16) for b in range(2)]
        evf = [sbx(esA, f"s_evf{b}", [P, 512], F32) for b in range(2)]
        evo = [sbx(esA, f"s_evo{b}", [P, 512], BF16) for b in range(2)]
        esA0 = es.enter_context(contextlib.ExitStack())
        xz = [sbx(esA0, f"s_xz{b}", [P, D], F32) for b in range(2)]
        xnz = sbx(esA0, "s_xnz", [P, D], F32)
        xqm = sbx(esA0, "s_xqm", [P, D], F32)
        xsq = xqm
        xq_bf = sbx(esA0, "s_xq_bf", [P, D], BF16)
        xn_bf = sbx(esA0, "s_xn_bf", [P, D], BF16)
        fmst = sbx(esA0, "s_fmst", [P, D], BF16)
        WQW = 2816

        if PHASE_LIMIT < 2:
            finish()
            return nc
        g_xz = [Guard(1), Guard(1)]
        g_tp = Guard(1)
        g_fmst = Guard(1)
        g_xbuf = Guard(1)
        cast_tok_k = [None] * 4
        next_k = [0, 0]
        g_kps = [Guard(1), Guard(1)]
        g_kcast = Guard(4)
        fm_toks = []
        wk_ld = []

        def rec_k_tile(tk, cs):
            c0 = cs * 512
            t0k = tk * P
            pps = [ps_a, ps_b][cs]
            be = cs

            @block.tensor
            def _(t):
                t.wait_ge(s_sc.h, cast_tok_k[cs])
                t.wait_ge(s_ve.h, fm_toks[tk])
                t.wait_ge(s_ve.h, g_kps[cs].tok())
                ins = None
                for dc in range(16):
                    ins = t.matmul(
                        pps[:, :],
                        xn_fm[:, :].rearrange("p (dc t) -> p dc t", dc=NT)[:, dc, t0k:t0k + P],
                        wsl[cs][:, dc * 512:(dc + 1) * 512],
                        start=(dc == 0), stop=(dc == 15))
                s_mm.inc(ins)
            mm_tok = s_mm.n

            @block.scalar
            def _(sc):
                sc.wait_ge(s_mm.h, mm_tok)
                sc.wait_ge(s_ve.h, state.get(f"p1dv_{be}", 0))
                sc.activation(out=ev1[be][:, :], in_=pps[:, :], func=AF.Exp)
                s_sc.inc(sc.activation(out=ev2[be][:, :], in_=ev1[be][:, :], func=AF.Ln,
                                       bias=np.float32(1.0)))
            sp_tok = s_sc.n

            @block.vector
            def _(v):
                v.wait_ge(s_sc.h, sp_tok)
                v.wait_ge(s_pst01[be].h, state.get(f"p1st_{be}", 0))
                v.tensor_copy(out=evh[be][:, :], in_=ev2[be][:, :])
                v.tensor_tensor(out=evf[be][:, :], in0=pps[:, :], in1=ev2[be][:, :],
                                op=ALU.subtract)
                s_ve.inc(v.tensor_tensor(out=evl[be][:, :], in0=ev2[be][:, :],
                                         in1=evh[be][:, :], op=ALU.subtract))
            state[f"p1dv_{be}"] = s_ve.n
            g_kps[cs].push(s_ve.n)

            @block.sync
            def _(sync):
                sync.wait_ge(s_ve.h, state[f"p1dv_{be}"])
                s_pst01[be].inc(sync.dma_start(out=d_d[t0k:t0k + P, c0:c0 + 512], in_=evf[be][:, :]), 16)
                s_pst01[be].inc(sync.dma_start(out=d_sph[t0k:t0k + P, c0:c0 + 512], in_=evh[be][:, :]), 16)
                s_pst01[be].inc(sync.dma_start(out=d_spl[t0k:t0k + P, c0:c0 + 512], in_=evl[be][:, :]), 16)
            state[f"p1st_{be}"] = s_pst01[be].n

        for tt in range(NT):
            b = tt % 2
            t0 = tt * P

            @block.sync
            def _(sync):
                sync.wait_ge(s_sc.h, g_xz[b].tok())
                s_x01[b].inc(sync.dma_start(out=xz[b][:], in_=ap_x[t0:t0 + P, :]), 16)
            xl_sem, xl_tok = s_x01[b], s_x01[b].n

            for j4 in range(4):
                ck = 4 * tt + j4
                cs_k, dc_k = ck // 16, ck % 16
                bwk = ck % 4

                @block.sync
                def _(sync):
                    sync.wait_ge(s_sc.h, g_kcast.tok())
                    s_wl01[bwk].inc(sync.dma_start(
                        out=wld[bwk][:, :],
                        in_=ap_wk[dc_k * P:(dc_k + 1) * P, cs_k * 512:(cs_k + 1) * 512]), 16)
                wk_ld.append(s_wl01[bwk].n)

            @block.vector
            def _(v):
                v.wait_ge(xl_sem.h, xl_tok)
                v.wait_ge(s_sc.h, state.get(f"p0xn_{tt-1}", 0))
                s_ve.inc(v.reduce_sum(out=st1[:, 0:1], in_=xz[b][:, :], axis=AX.X))
            s1_tok = s_ve.n

            @block.vector
            def _(v):
                v.tensor_tensor(out=xsq[:, :], in0=xz[b][:, :], in1=xz[b][:, :], op=ALU.mult)
                v.reduce_sum(out=st2[:, 0:1], in_=xsq[:, :], axis=AX.X).then_inc(s_ve.h, 1)
                s_ve.n += 1
                v.tensor_scalar(out=st1[:, 0:1], in0=st1[:, 0:1], scalar1=float(1.0 / D),
                                scalar2=None, op0=ALU.mult).then_inc(s_ve.h, 1)
                s_ve.n += 1
                v.wait_ge(s_ve.h, s_ve.n)
                v.tensor_tensor(out=st3[:, 0:1], in0=st1[:, 0:1], in1=st1[:, 0:1], op=ALU.mult)
                v.tensor_scalar(out=st2[:, 0:1], in0=st2[:, 0:1], scalar1=float(1.0 / D),
                                scalar2=None, op0=ALU.mult).then_inc(s_ve.h, 1)
                s_ve.n += 1
                v.wait_ge(s_ve.h, s_ve.n)
                s_ve.inc(v.tensor_tensor(out=st2[:, 0:1], in0=st2[:, 0:1], in1=st3[:, 0:1],
                                         op=ALU.subtract))
            var_tok = s_ve.n

            if KDBG:
                @block.sync
                def _(sync):
                    sync.wait_ge(s_ve.h, var_tok)
                    s_st.inc(sync.dma_start(out=d_stats[:, 8 * tt:8 * tt + 1], in_=st1[:, 0:1], allow_slow_non_contiguous=True), 16)
                    s_st.inc(sync.dma_start(out=d_stats[:, 8 * tt + 1:8 * tt + 2], in_=st2[:, 0:1], allow_slow_non_contiguous=True), 16)

            @block.scalar
            def _(sc):
                sc.wait_ge(s_ve.h, var_tok)
                s_sc.inc(sc.activation(out=st3[:, 0:1], in_=st2[:, 0:1], func=AF.Sqrt,
                                       bias=np.float32(LN_EPS)))
            std_tok = s_sc.n

            @block.vector
            def _(v):
                v.wait_ge(s_sc.h, std_tok)
                v.reciprocal(out=st4[:, 0:1], in_=st3[:, 0:1]).then_inc(s_ve.h, 1)
                s_ve.n += 1
                v.wait_ge(s_ve.h, s_ve.n)
                s_ve.inc(v.scalar_tensor_tensor(out=st3[:, 0:1], in0=st1[:, 0:1], scalar=-1.0,
                                                in1=st4[:, 0:1], op0=ALU.mult, op1=ALU.mult))
            rstd_tok = s_ve.n

            if KDBG:
                @block.sync
                def _(sync):
                    sync.wait_ge(s_ve.h, rstd_tok)
                    s_st.inc(sync.dma_start(out=d_stats[:, 8 * tt + 2:8 * tt + 3], in_=st4[:, 0:1], allow_slow_non_contiguous=True), 16)
                    s_st.inc(sync.dma_start(out=d_stats[:, 8 * tt + 3:8 * tt + 4], in_=st3[:, 0:1], allow_slow_non_contiguous=True), 16)

            @block.scalar
            def _(sc):
                sc.wait_ge(s_ve.h, rstd_tok)
                s_sc.inc(sc.activation(out=xnz[:, :], in_=xz[b][:, :], func=AF.Identity,
                                       scale=st4[:, 0:1], bias=st3[:, 0:1]))
            state[f"p0xn_{tt}"] = s_sc.n
            g_xz[b].push(s_sc.n)
            xn_tok = s_sc.n

            @block.vector
            def _(v):
                v.wait_ge(s_sc.h, xn_tok)
                v.wait_ge(s_mm.h, g_xbuf.tok())
                v.tensor_reduce(out=st1[:, 0:1], in_=xnz[:, :], axis=AX.X, op=ALU.max,
                                apply_absolute_value=True).then_inc(s_ve.h, 1)
                s_ve.n += 1
                v.wait_ge(s_ve.h, s_ve.n)
                v.tensor_scalar(out=st1[:, 0:1], in0=st1[:, 0:1], scalar1=float(Q_EPS),
                                scalar2=None, op0=ALU.max).then_inc(s_ve.h, 1)
                s_ve.n += 1
                v.wait_ge(s_ve.h, s_ve.n)
                v.reciprocal(out=st2[:, 0:1], in_=st1[:, 0:1]).then_inc(s_ve.h, 1)
                s_ve.n += 1
                v.wait_ge(s_ve.h, s_ve.n)
                v.tensor_scalar(out=sx_all[:, tt:tt + 1], in0=st2[:, 0:1], scalar1=127.0,
                                scalar2=None, op0=ALU.mult)
                v.tensor_scalar(out=sxi_all[:, tt:tt + 1], in0=st1[:, 0:1],
                                scalar1=float(1.0 / 127.0), scalar2=None, op0=ALU.mult)
                s_ve.inc(v.tensor_copy(out=xn_bf[:, :], in_=xnz[:, :]))
            sx_tok = s_ve.n

            if KDBG:
                @block.sync
                def _(sync):
                    sync.wait_ge(s_ve.h, sx_tok)
                    s_st.inc(sync.dma_start(out=d_stats[:, 8 * tt + 4:8 * tt + 5], in_=sx_all[:, tt:tt + 1], allow_slow_non_contiguous=True), 16)
                    s_st.inc(sync.dma_start(out=d_stats[:, 8 * tt + 5:8 * tt + 6], in_=st1[:, 0:1], allow_slow_non_contiguous=True), 16)

            @block.scalar
            def _(sc):
                sc.wait_ge(s_ve.h, sx_tok)
                s_sc.inc(sc.activation(out=xqm[:, :], in_=xnz[:, :], func=AF.Identity,
                                       scale=sx_all[:, tt:tt + 1], bias=np.float32(MAGIC)))
            xqm_tok = s_sc.n

            @block.vector
            def _(v):
                v.wait_ge(s_sc.h, xqm_tok)
                s_ve.inc(v.tensor_scalar(out=xq_bf[:, :], in0=xqm[:, :], scalar1=MAGIC,
                                         scalar2=None, op0=ALU.subtract))
            xqb_tok = s_ve.n

            last_tp_mm = 0
            for which, (src_t, src_tok) in enumerate([(xn_bf, sx_tok), (xq_bf, xqb_tok)]):
                for r in range(2):
                    @block.tensor
                    def _(t):
                        t.wait_ge(s_ve.h, src_tok)
                        t.wait_ge(s_ve.h, g_tp.tok())
                        if tt == 0 and which == 0 and r == 0:
                            t.wait_ge(s_ld.h, const_tok)
                        ins = None
                        for j in range(8):
                            dc = r * 8 + j
                            ins = t.transpose(ps_tp[:, j * P:(j + 1) * P],
                                              src_t[:, dc * P:(dc + 1) * P], ident[:, :])
                        s_mm.inc(ins)
                    tp_tok = s_mm.n
                    last_tp_mm = tp_tok

                    @block.vector
                    def _(v):
                        v.wait_ge(s_mm.h, tp_tok)
                        if which == 0:
                            s_ve.inc(v.tensor_copy(
                                out=xn_fm[:, :].rearrange("p (dc t) -> p dc t", dc=NT)[:, r * 8:(r + 1) * 8, t0:t0 + P],
                                in_=ps_tp[:, :].rearrange("p (j t) -> p j t", j=8)))
                        else:
                            if r == 0:
                                v.wait_ge(s_xqf.h, g_fmst.tok())
                            s_ve.inc(v.tensor_copy(
                                out=fmst[:, r * 1024:(r + 1) * 1024], in_=ps_tp[:, :]))
                    g_tp.push(s_ve.n)
            fm_tok = s_ve.n
            g_xbuf.push(last_tp_mm)

            @block.sync
            def _(sync):
                sync.wait_ge(s_ve.h, fm_tok)
                s_xqf.inc(sync.dma_start(out=d_xqf[t0:t0 + P, :], in_=fmst[:, :]), 16)
            g_fmst.push(s_xqf.n)
            fm_toks.append(fm_tok)

            for j4 in range(4):
                ck = 4 * tt + j4
                cs_k, dc_k = ck // 16, ck % 16
                bwk = ck % 4

                @block.scalar
                def _(sc):
                    sc.wait_ge(s_wl01[bwk].h, wk_ld[ck])
                    s_sc.inc(sc.copy(out=wsl[cs_k][:, dc_k * 512:(dc_k + 1) * 512],
                                     in_=wld[bwk][:, :]))
                g_kcast.push(s_sc.n)
                if dc_k == 15:
                    @block.scalar
                    def _(sc):
                        s_sc.inc(sc.copy(out=stk[:, 0:1], in_=wsl[cs_k][:, 0:1]))
                    cast_tok_k[cs_k] = s_sc.n

            for cs_k in (0, 1):
                if cast_tok_k[cs_k] is None:
                    continue
                while next_k[cs_k] <= tt:
                    rec_k_tile(next_k[cs_k], cs_k)
                    next_k[cs_k] += 1

        join()

        block.flush()
        if PHASE_LIMIT < 3:
            finish()
            return nc
        esA0.close()

        def weight_pass(apw, kind, cs_list=(0, 1, 2, 3), cast_toks=None, seed=None):
            g_ps = [Guard(1), Guard(1)]
            if seed is not None:
                g_ps[0].hist = [seed[0]]
                g_ps[1].hist = [seed[1]]
            g_cast = Guard(2)
            g_evA = Guard(2)
            for cs in cs_list:
                c0 = cs * 512
                bcs = cs
                for dc in (range(16) if cast_toks is None else []):
                    bw2 = dc % 2

                    @block.sync
                    def _(sync):
                        sync.wait_ge(s_sc.h, g_cast.tok())
                        s_wl01[bw2].inc(sync.dma_start(out=wld[bw2][:, :],
                                                in_=apw[dc * P:(dc + 1) * P, c0:c0 + 512]), 16)
                    lt_sem, lt = s_wl01[bw2], s_wl01[bw2].n

                    @block.scalar
                    def _(sc):
                        sc.wait_ge(lt_sem.h, lt)
                        s_sc.inc(sc.copy(out=wsl[bcs][:, dc * 512:(dc + 1) * 512],
                                         in_=wld[bw2][:, :]))
                    g_cast.push(s_sc.n)
                if cast_toks is None:
                    @block.scalar
                    def _(sc):
                        s_sc.inc(sc.copy(out=st1[:, 0:1], in_=wsl[bcs][:, 0:1]))
                    cast_tok = s_sc.n
                else:
                    cast_tok = cast_toks[cs]
                if KDBG and kind == "v":
                    @block.sync
                    def _(sync):
                        sync.wait_ge(s_sc.h, cast_tok)
                        s_st.inc(sync.dma_start(out=d_wsl[cs * P:(cs + 1) * P, :],
                                                in_=wsl[bcs][:, :]), 16)

                for tt in range(NT):
                    pps = [ps_a, ps_b][tt % 2]
                    gi = tt % 2
                    t0 = tt * P

                    @block.tensor
                    def _(t):
                        t.wait_ge(s_sc.h, cast_tok)
                        if kind == "k":
                            t.wait_ge(s_ve.h, g_ps[gi].tok())
                        else:
                            t.wait_ge(s_sc.h, g_ps[gi].tok())
                        for dc in range(16):
                            ins = t.matmul(
                                pps[:, :],
                                xn_fm[:, :].rearrange("p (dc t) -> p dc t", dc=NT)[:, dc, t0:t0 + P],
                                wsl[bcs][:, dc * 512:(dc + 1) * 512],
                                start=(dc == 0), stop=(dc == 15))
                        s_mm.inc(ins)
                    state[f"wsl_{kind}_{cs}"] = s_mm.n
                    mm_tok = s_mm.n

                    if kind == "k":
                        be = tt % 2

                        @block.scalar
                        def _(sc):
                            sc.wait_ge(s_mm.h, mm_tok)
                            sc.wait_ge(s_ve.h, state.get(f"p1dv_{be}", 0))
                            sc.activation(out=ev1[be][:, :], in_=pps[:, :], func=AF.Exp)
                            s_sc.inc(sc.activation(out=ev2[be][:, :], in_=ev1[be][:, :], func=AF.Ln,
                                                   bias=np.float32(1.0)))
                        sp_tok = s_sc.n

                        @block.vector
                        def _(v):
                            v.wait_ge(s_sc.h, sp_tok)
                            v.wait_ge(s_pst01[be].h, state.get(f"p1st_{be}", 0))
                            v.tensor_copy(out=evh[be][:, :], in_=ev2[be][:, :])
                            v.tensor_tensor(out=evf[be][:, :], in0=pps[:, :], in1=ev2[be][:, :],
                                            op=ALU.subtract)
                            s_ve.inc(v.tensor_tensor(out=evl[be][:, :], in0=ev2[be][:, :],
                                                     in1=evh[be][:, :], op=ALU.subtract))
                        state[f"p1dv_{be}"] = s_ve.n
                        g_ps[gi].push(s_ve.n)
                        dv_tok = s_ve.n

                        @block.sync
                        def _(sync):
                            sync.wait_ge(s_ve.h, dv_tok)
                            s_pst01[be].inc(sync.dma_start(out=d_d[t0:t0 + P, c0:c0 + 512], in_=evf[be][:, :]), 16)
                            s_pst01[be].inc(sync.dma_start(out=d_sph[t0:t0 + P, c0:c0 + 512], in_=evh[be][:, :]), 16)
                            s_pst01[be].inc(sync.dma_start(out=d_spl[t0:t0 + P, c0:c0 + 512], in_=evl[be][:, :]), 16)
                        state[f"p1st_{be}"] = s_pst01[be].n
                    else:
                        dst = d_v if kind == "v" else d_q
                        be = tt % 2

                        @block.scalar
                        def _(sc):
                            sc.wait_ge(s_mm.h, mm_tok)
                            sc.wait_ge(s_pso01[be].h, state.get(f"p1st2_{kind}_{be}", 0))
                            fn = AF.Copy if kind == "v" else AF.Relu
                            s_sc.inc(sc.activation(out=evo[be][:, :], in_=pps[:, :], func=fn))
                        g_ps[gi].push(s_sc.n)
                        ev_tok = s_sc.n

                        @block.sync
                        def _(sync):
                            sync.wait_ge(s_sc.h, ev_tok)
                            s_pso01[be].inc(sync.dma_start(out=dst[t0:t0 + P, c0:c0 + 512],
                                                    in_=evo[be][:, :]), 16)
                        state[f"p1st2_{kind}_{be}"] = s_pso01[be].n

        weight_pass(ap_wk, "k", cs_list=(2, 3), cast_toks=cast_tok_k,
                    seed=(state["p1dv_0"], state["p1dv_1"]))
        join()
        block.flush()
        weight_pass(ap_wi, "v")
        join()
        block.flush()
        weight_pass(ap_wq, "q")
        join()

        block.flush()
        esA.close()
        esB = es.enter_context(contextlib.ExitStack())
        sphi = [sbx(esB, f"s_sphi{b}", [P, D], BF16) for b in range(2)]
        splo = [sbx(esB, f"s_splo{b}", [P, D], BF16) for b in range(2)]
        dt_ = [sbx(esB, f"s_dt{b}", [P, D], F32) for b in range(2)]
        qin = [sbx(esB, f"s_qin{b}", [P, D], BF16) for b in range(2)]
        vt_ = [sbx(esB, f"s_vt{b}", [P, D], BF16) for b in range(2)]
        ka_tm = sbx(esB, "s_ka_tm", [P, D], BF16)
        ks_tm = sbx(esB, "s_ks_tm", [P, D], BF16)
        qt_tm = sbx(esB, "s_qt_tm", [P, D], BF16)
        eq_sc = sbx(esB, "s_eq_sc", [P, 512], F32)
        tsc = sbx(esB, "s_tsc", [P, 512], F32)
        tsc2 = sbx(esB, "s_tsc2", [P, 1024], F32)
        ka_fm = sbx(esB, "s_ka_fm", [P, D], BF16)
        qt_fm = sbx(esB, "s_qt_fm", [P, D], BF16)
        qh_fm = sbx(esB, "s_qh_fm", [P, D], BF16)
        pm_sb = sbx(esB, "s_pm_sb", [P, 1024], BF16)
        a_sb = sbx(esB, "s_a_sb", [P, 32], F32)
        ea_sb = sbx(esB, "s_ea_sb", [P, 32], F32)
        er_sb = sbx(esB, "s_er_sb", [P, NH], F32)
        rn_sb = sbx(esB, "s_rn_sb", [P, NH], F32)
        s_sb = sbx(esB, "s_s_sb", [P, D], F32)
        sbf_sb = sbx(esB, "s_sbf_sb", [P, D], BF16)
        o_st = sbx(esB, "s_o_st", [P, 1024], BF16)
        pubst = sbx(esB, "s_pubst", [P, 2064], F32)

        esW = es.enter_context(contextlib.ExitStack())
        wql = [sbx(esW, f"s_wql{b}", [P, WQW], F32) for b in range(2)]
        wqt = sbx(esW, "s_wqt", [P, WQW], F32)
        wqb = [sbx(esW, f"s_wqb{b}", [P, WQW], BF16) for b in range(2)]
        wtiles = ([("w1", ap_w1own, r * P, P, c * WQW, WQW) for r in range(2) for c in range(2)] +
                  [("w3", ap_w3own, r * P, P, c * WQW, WQW) for r in range(2) for c in range(2)] +
                  [("w2", ap_w2own, r * P, P, 0, D) for r in range(5)] +
                  [("w2", ap_w2own, 5 * P, 64, 0, D)])

        @block.vector
        def _(v):
            v.memset(wpart[:, :], 0.0)
            v.memset(ones1f[:, :], 1.0)
            s_ve.inc(v.tensor_copy(out=st1[:, 0:1], in_=wpart[:, 0:1]))
        wp0_tok = s_ve.n

        g_wql = [Guard(1), Guard(1)]
        for idx, (wname, apw, r0, rows, c0w, cols) in enumerate(wtiles):
            bw = idx % 2

            @block.sync
            def _(sync):
                sync.wait_ge(s_sc.h, g_wql[bw].tok())
                s_ld.inc(sync.dma_start(out=wql[bw][0:rows, 0:cols],
                                        in_=apw[r0:r0 + rows, c0w:c0w + cols]), 16)
            wl_tok = s_ld.n

            @block.scalar
            def _(sc):
                sc.wait_ge(s_ld.h, wl_tok)
                if idx == 0:
                    sc.wait_ge(s_ve.h, wp0_tok)
                sc.wait_ge(s_ve.h, state.get("wabs_red", 0))
                s_sc.inc(sc.activation(out=wqt[0:rows, 0:cols], in_=wql[bw][0:rows, 0:cols],
                                       func=AF.Abs))
            g_wql[bw].push(s_sc.n)
            wabs_sc = s_sc.n

            @block.vector
            def _(v):
                v.wait_ge(s_sc.h, wabs_sc)
                s_ve.inc(v.reduce_sum(out=wpart[0:rows, idx:idx + 1], in_=wqt[0:rows, 0:cols],
                                      axis=AX.X))
            state["wabs_red"] = s_ve.n
        wabs_tok = s_ve.n
        wabs_sc_final = s_sc.n

        @block.tensor
        def _(t):
            t.wait_ge(s_ve.h, wabs_tok)
            t.wait_ge(s_ld.h, const_tok)
            s_mm.inc(t.matmul(ps_a[0:1, 0:16], ones1f[:, 0:1], wpart[:, :], start=True, stop=True))
        wred_tok = s_mm.n

        @block.vector
        def _(v):
            v.wait_ge(s_mm.h, wred_tok)
            s_ve.inc(v.tensor_copy(out=wsums[0:1, 0:16], in_=ps_a[0:1, 0:16]))
        wsum_tok = s_ve.n

        @block.sync
        def _(sync):
            sync.wait_ge(s_ve.h, wsum_tok)
            s_st.inc(sync.dma_start(out=ar_in[0:1, :], in_=wsums[0:1, :]), 16)
        arin_tok = s_st.n

        @block.gpsimd
        def _(g):
            g.wait_ge(s_st.h, arin_tok)
            s_cc.inc(g.collective_compute("AllReduce", ALU.add, replica_groups=RG,
                                          ins=[ar_in.opt()], outs=[ar_out.opt()]))
        ar_tok = s_cc.n

        @block.sync
        def _(sync):
            sync.wait_ge(s_cc.h, ar_tok)
            s_ld.inc(sync.dma_start(out=warb[:, 0:16], in_=ar_out[0:1, 0:16].broadcast_to([P, 16])), 16)
        arld_tok = s_ld.n

        @block.vector
        def _(v):
            v.wait_ge(s_ld.h, arld_tok)
            v.reduce_sum(out=warb2[:, 0:1], in_=warb[:, 0:4], axis=AX.X).then_inc(s_ve.h, 1)
            v.reduce_sum(out=warb2[:, 1:2], in_=warb[:, 4:8], axis=AX.X).then_inc(s_ve.h, 1)
            v.reduce_sum(out=warb2[:, 2:3], in_=warb[:, 8:14], axis=AX.X).then_inc(s_ve.h, 1)
            s_ve.n += 3
            v.wait_ge(s_ve.h, s_ve.n)
            v.tensor_scalar(out=warb2[:, 4:7], in0=warb2[:, 0:3],
                            scalar1=float(1.0 / (D * DH)), scalar2=float(Q_EPS),
                            op0=ALU.mult, op1=ALU.max).then_inc(s_ve.h, 1)
            s_ve.n += 1
            v.wait_ge(s_ve.h, s_ve.n)
            v.reciprocal(out=wscl[:, 0:3], in_=warb2[:, 4:7]).then_inc(s_ve.h, 1)
            s_ve.n += 1
            v.wait_ge(s_ve.h, s_ve.n)
            v.tensor_scalar(out=wscl[:, 3:4], in0=warb2[:, 6:7],
                            scalar1=float(1.0 / 127.0), scalar2=None, op0=ALU.mult)
            v.tensor_copy(out=wscl[:, 4:6], in_=warb2[:, 4:6]).then_inc(s_ve.h, 1)
            s_ve.n += 1
            v.wait_ge(s_ve.h, s_ve.n)
            s_ve.inc(v.tensor_copy(out=wsums[0:1, 0:8], in_=wscl[0:1, 0:8]))
        wbc_hold = s_ve.n

        if KDBG:
            @block.sync
            def _(sync):
                sync.wait_ge(s_ve.h, wbc_hold)
                s_st.inc(sync.dma_start(out=d_warb[:, 0:16], in_=warb[:, :]), 16)
                s_st.inc(sync.dma_start(out=d_warb[:, 16:24], in_=warb2[:, :]), 16)

        @block.sync
        def _(sync):
            sync.wait_ge(s_ve.h, wbc_hold)
            s_st.inc(sync.dma_start(out=d_scl[0:1, 0:16], in_=wsums[0:1, 0:16]), 16)
        wbc_tok2 = s_st.n
        wbc_tok = s_ld.n
        state["wscl_ready"] = wbc_hold

        g_wqlB = Guard(2)
        g_wqbB = Guard(2)
        for idx, (wname, apw, r0, rows, c0w, cols) in enumerate(wtiles):
            bw = idx % 2
            scol = {"w1": 0, "w3": 1, "w2": 2}[wname]
            dst = {"w1": wag1_in, "w3": wag3_in, "w2": wag2_in}[wname]

            @block.sync
            def _(sync):
                sync.wait_ge(s_sc.h, max(g_wqlB.tok(), wabs_sc_final))
                s_wq01[bw].inc(sync.dma_start(out=wql[bw][0:rows, 0:cols],
                                        in_=apw[r0:r0 + rows, c0w:c0w + cols]), 16)
            wl_sem, wl_tok = s_wq01[bw], s_wq01[bw].n

            @block.scalar
            def _(sc):
                sc.wait_ge(wl_sem.h, wl_tok)
                if idx == 0:
                    sc.wait_ge(s_ve.h, state["wscl_ready"])
                sc.wait_ge(s_ve.h, wq3_toks[-1] if wq3_toks else 0)
                s_sc.inc(sc.activation(out=wqt[0:rows, 0:cols], in_=wql[bw][0:rows, 0:cols],
                                       func=AF.Identity, scale=wscl[0:rows, scol:scol + 1],
                                       bias=np.float32(MAGIC)))
            g_wqlB.push(s_sc.n)
            wq2_tok = s_sc.n

            @block.vector
            def _(v):
                v.wait_ge(s_sc.h, wq2_tok)
                v.wait_ge(s_st.h, g_wqbB.tok())
                v.tensor_scalar(out=wqt[0:rows, 0:cols], in0=wqt[0:rows, 0:cols],
                                scalar1=MAGIC, scalar2=1.0, op0=ALU.subtract, op1=ALU.min)
                s_ve.inc(v.tensor_scalar(out=wqb[bw][0:rows, 0:cols], in0=wqt[0:rows, 0:cols],
                                         scalar1=-1.0, scalar2=None, op0=ALU.max))
            wq3_toks.append(s_ve.n)
            wq3_tok = s_ve.n

            @block.sync
            def _(sync):
                sync.wait_ge(s_ve.h, wq3_tok)
                s_st.inc(sync.dma_start(out=dst[r0:r0 + rows, c0w:c0w + cols],
                                        in_=wqb[bw][0:rows, 0:cols]), 16)
            g_wqbB.push(s_st.n)
        wqdone_tok = s_st.n

        @block.gpsimd
        def _(g):
            g.wait_ge(s_st.h, wqdone_tok)
            s_cc.inc(g.collective_compute("AllGather", ALU.bypass, replica_groups=RG,
                                          ins=[wag1_in.opt()], outs=[wag1_out.opt()]))
            s_cc.inc(g.collective_compute("AllGather", ALU.bypass, replica_groups=RG,
                                          ins=[wag3_in.opt()], outs=[wag3_out.opt()]))
            s_cc.inc(g.collective_compute("AllGather", ALU.bypass, replica_groups=RG,
                                          ins=[wag2_in.opt()], outs=[wag2_out.opt()]))
        wag_tok = s_cc.n
        @block.vector
        def _(v):
            v.wait_ge(s_ve.h, state["wscl_ready"])
            v.tensor_tensor(out=rs1_all[:, :], in0=sxi_all[:, :],
                            in1=wscl[:, 4:5].broadcast_to([P, NT]), op=ALU.mult)
            s_ve.inc(v.tensor_tensor(out=rs3_all[:, :], in0=sxi_all[:, :],
                                     in1=wscl[:, 5:6].broadcast_to([P, NT]), op=ALU.mult))
        rsff_tok = s_ve.n

        block.flush()
        if PHASE_LIMIT < 4:
            finish()
            return nc

        @block.vector
        def _(v):
            v.memset(s_sb[:, :], 0.0)
            v.memset(rn_sb[:, :], 0.0)
            s_ve.inc(v.tensor_copy(out=sbf_sb[:], in_=s_sb[:]))

        p2 = dict(last_ks=0, last_qt=0, prev_er=0, prev_tend=0,
                  prev_ost=0, prev_qh=0, cs_cons=[0, 0], cs_cons_sc=[0, 0],
                  sa_cons=0, p_cons=0, o_cons=0, tp_cons=0)

        for i in range(NT):
            b = i % 2
            t0 = i * P

            @block.sync
            def _(sync):
                if i >= 2:
                    sync.wait_ge(s_fr.h, i - 1)
                s_p2l01[b].inc(sync.dma_start(out=sphi[b][:], in_=d_sph[t0:t0 + P, :]), 16)
                s_p2l01[b].inc(sync.dma_start(out=splo[b][:], in_=d_spl[t0:t0 + P, :]), 16)
                s_p2l01[b].inc(sync.dma_start(out=dt_[b][:], in_=d_d[t0:t0 + P, :]), 16)
                s_p2l01[b].inc(sync.dma_start(out=qin[b][:], in_=d_q[t0:t0 + P, :]), 16)
                s_p2l01[b].inc(sync.dma_start(out=vt_[b][:], in_=d_v[t0:t0 + P, :]), 16)
            ld_sem, ld_tok = s_p2l01[b], s_p2l01[b].n
            prev_fence = s_mm.n

            for s in range(4):
                sl = slice(512 * s, 512 * (s + 1))
                pcs = [ps_a, ps_b][s % 2]

                @block.tensor
                def _(t):
                    t.wait_ge(ld_sem.h, ld_tok)
                    t.wait_ge(s_ve.h, p2["cs_cons"][s % 2])
                    t.wait_ge(s_sc.h, p2["cs_cons_sc"][s % 2])
                    t.matmul(pcs[:, :], tri_i[:, :], sphi[b][:, sl], start=True, stop=False)
                    s_mm.inc(t.matmul(pcs[:, :], tri_i[:, :], splo[b][:, sl], start=False, stop=True))
                csp_tok = s_mm.n

                @block.vector
                def _(v):
                    v.wait_ge(s_mm.h, csp_tok)
                    v.wait_ge(s_sc.h, p2["last_ks"])
                    s_ve.inc(v.tensor_tensor(out=tsc[:, :], in0=pcs[:, :], in1=dt_[b][:, sl], op=ALU.add))
                tka_tok = s_ve.n

                @block.scalar
                def _(sc):
                    sc.wait_ge(s_ve.h, tka_tok)
                    sc.wait_ge(s_ve.h, p2["last_qt"])
                    sc.wait_ge(s_mm.h, prev_fence)
                    sc.activation(out=ka_tm[:, sl], in_=tsc[:, :], func=AF.Exp)
                    s_sc.inc(sc.activation(out=eq_sc[:, :], in_=pcs[:, :], func=AF.Exp,
                                           scale=-1.0, bias=np.float32(-C0)))
                eq_tok = s_sc.n

                @block.vector
                def _(v):
                    v.wait_ge(s_sc.h, eq_tok)
                    s_ve.inc(v.tensor_tensor(out=qt_tm[:, sl], in0=qin[b][:, sl], in1=eq_sc[:, :], op=ALU.mult))
                p2["last_qt"] = s_ve.n

                @block.tensor
                def _(t):
                    t.wait_ge(s_ve.h, tka_tok)
                    t.wait_ge(s_sc.h, eq_tok)
                    t.matmul(pcs[:, :], tri_x[:, :], sphi[b][:, sl], start=True, stop=False)
                    s_mm.inc(t.matmul(pcs[:, :], tri_x[:, :], splo[b][:, sl], start=False, stop=True))
                csx_tok = s_mm.n

                @block.vector
                def _(v):
                    v.wait_ge(s_mm.h, csx_tok)
                    s_ve.inc(v.tensor_tensor(out=tsc[:, :], in0=dt_[b][:, sl], in1=pcs[:, :], op=ALU.subtract))
                tks_tok = s_ve.n
                p2["cs_cons"][s % 2] = tks_tok

                @block.scalar
                def _(sc):
                    sc.wait_ge(s_ve.h, tks_tok)
                    s_sc.inc(sc.activation(out=ks_tm[:, sl], in_=tsc[:, :], func=AF.Exp))
                p2["cs_cons_sc"][s % 2] = s_sc.n
                p2["last_ks"] = s_sc.n
            last_qtl = p2["last_qt"]

            @block.tensor
            def _(t):
                t.wait_ge(s_ve.h, p2["sa_cons"])
                t.wait_ge(ld_sem.h, ld_tok)
                ins = None
                for h in range(NH):
                    t.matmul(ps_c[:, h * 2:h * 2 + 2], sphi[b][:, h * P:(h + 1) * P],
                             csel[:, 0:2], start=True, stop=False)
                    ins = t.matmul(ps_c[:, h * 2:h * 2 + 2], splo[b][:, h * P:(h + 1) * P],
                                   csel[:, 0:2], start=False, stop=True)
                s_mm.inc(ins)
            sa_tok = s_mm.n

            @block.vector
            def _(v):
                v.wait_ge(s_mm.h, sa_tok)
                v.wait_ge(s_sc.h, p2["prev_er"])
                s_ve.inc(v.tensor_copy(out=a_sb[:, :].rearrange("p (c h) -> p c h", c=2),
                                       in_=ps_c[:, 0:32].rearrange("p (h c) -> p c h", h=NH)))
            a_tok = s_ve.n
            p2["sa_cons"] = a_tok

            @block.scalar
            def _(sc):
                sc.wait_ge(s_ve.h, a_tok)
                sc.wait_ge(s_ve.h, p2["prev_tend"])
                sc.activation(out=ea_sb[:, :], in_=a_sb[:, :], func=AF.Exp, scale=-1.0)
                s_sc.inc(sc.activation(out=er_sb[:, :], in_=rn_sb[:, :], func=AF.Exp, scale=-1.0))
            er_tok = s_sc.n
            p2["prev_er"] = er_tok

            for src_t, dst_t in [(qt_tm, qt_fm), (ka_tm, ka_fm)]:
                for r in range(2):
                    @block.tensor
                    def _(t):
                        if src_t is qt_tm:
                            t.wait_ge(s_ve.h, last_qtl)
                        else:
                            t.wait_ge(s_sc.h, p2["cs_cons_sc"][1])
                        t.wait_ge(s_ve.h, p2["tp_cons"])
                        ins = None
                        for j in range(8):
                            h = r * 8 + j
                            ins = t.transpose(ps_tp[:, j * P:(j + 1) * P],
                                              src_t[:, h * P:(h + 1) * P], ident[:, :])
                        s_mm.inc(ins)
                    tp_tok = s_mm.n

                    @block.vector
                    def _(v):
                        v.wait_ge(s_mm.h, tp_tok)
                        s_ve.inc(v.tensor_copy(out=dst_t[:, r * 1024:(r + 1) * 1024], in_=ps_tp[:, :]))
                    p2["tp_cons"] = s_ve.n
            fm_tok = s_ve.n

            @block.vector
            def _(v):
                v.wait_ge(s_sc.h, er_tok)
                v.wait_ge(s_qhst.h, p2["prev_qh"])
                v.tensor_tensor(
                    out=qh_fm[:, :].rearrange("p (h t) -> p h t", h=NH),
                    in0=qt_fm[:, :].rearrange("p (h t) -> p h t", h=NH),
                    in1=er_sb[:, :].rearrange("p (h o) -> p h o", o=1).broadcast_to([P, NH, P]),
                    op=ALU.mult)
                s_ve.inc(v.tensor_tensor(
                    out=qh_fm[:, :].rearrange("p (h c t) -> p h c t", h=NH, c=2)[:, :, 1:2, :],
                    in0=qh_fm[:, :].rearrange("p (h c t) -> p h c t", h=NH, c=2)[:, :, 1:2, :],
                    in1=ea_sb[:, 0:NH].rearrange("p (h o) -> p h o", o=1)
                        .rearrange("p h (a o) -> p h a o", a=1).broadcast_to([P, NH, 1, C]),
                    op=ALU.mult))
            qh_tok = s_ve.n

            @block.sync
            def _(sync):
                sync.wait_ge(s_ve.h, qh_tok)
                s_qhst.inc(sync.dma_start(out=d_qh[:, i * D:(i + 1) * D], in_=qh_fm[:, :]), 16)
            p2["prev_qh"] = s_qhst.n

            for c in range(2):
                cb = c * C
                ch = 2 * i + c

                @block.tensor
                def _(t):
                    t.wait_ge(s_ve.h, fm_tok)
                    t.wait_ge(s_ve.h, p2["p_cons"])
                    ins = None
                    for h in range(NH):
                        ins = t.matmul(ps_e2[cb:cb + C, h * C:(h + 1) * C],
                                       ka_fm[:, h * P + cb:h * P + cb + C],
                                       qt_fm[:, h * P + cb:h * P + cb + C],
                                       start=True, stop=True)
                    s_mm.inc(ins)
                pt_tok = s_mm.n

                @block.vector
                def _(v):
                    v.wait_ge(s_mm.h, pt_tok)
                    s_ve.inc(v.scalar_tensor_tensor(out=pm_sb[cb:cb + C, :], in0=ps_e2[cb:cb + C, :],
                                                    scalar=1.0, in1=maskp[cb:cb + C, :],
                                                    op0=ALU.bypass, op1=ALU.mult))
                pm_tok = s_ve.n
                p2["p_cons"] = pm_tok

                @block.tensor
                def _(t):
                    t.wait_ge(s_ve.h, pm_tok)
                    t.wait_ge(s_ve.h, p2["o_cons"])
                    ins = None
                    for h in range(NH):
                        t.matmul(ps_d2[:, h * C:(h + 1) * C],
                                 vt_[b][cb:cb + C, h * P:(h + 1) * P],
                                 pm_sb[cb:cb + C, h * C:(h + 1) * C],
                                 start=True, stop=False)
                        ins = t.matmul(ps_d2[:, h * C:(h + 1) * C],
                                       sbf_sb[:, h * P:(h + 1) * P],
                                       qt_fm[:, h * P + cb:h * P + cb + C],
                                       start=False, stop=True)
                    s_mm.inc(ins)
                o_tok = s_mm.n

                @block.vector
                def _(v):
                    v.wait_ge(s_mm.h, o_tok)
                    v.wait_ge(s_ost.h, p2["prev_ost"])
                    s_ve.inc(v.tensor_copy(out=o_st[:, :], in_=ps_d2[:, :]))
                oev_tok = s_ve.n
                p2["o_cons"] = oev_tok

                @block.sync
                def _(sync):
                    sync.wait_ge(s_ve.h, oev_tok)
                    s_ost.inc(sync.dma_start(out=d_ol[ch * P:(ch + 1) * P, :], in_=o_st[:, :]), 16)
                p2["prev_ost"] = s_ost.n

                for hh in range(2):
                    hb = hh * 8

                    @block.tensor
                    def _(t):
                        t.wait_ge(s_ve.h, p2["o_cons"])
                        t.wait_ge(s_sc.h, p2["cs_cons_sc"][1])
                        ins = None
                        for j in range(8):
                            h = hb + j
                            ins = t.matmul(ps_d2[:, j * P:(j + 1) * P],
                                           ks_tm[cb:cb + C, h * P:(h + 1) * P],
                                           vt_[b][cb:cb + C, h * P:(h + 1) * P],
                                           start=True, stop=True)
                        s_mm.inc(ins)
                    u_tok = s_mm.n

                    @block.vector
                    def _(v):
                        v.wait_ge(s_mm.h, u_tok)
                        hsl = slice(hb * P, (hb + 8) * P)
                        v.tensor_tensor(
                            out=tsc2[:, :].rearrange("p (h t) -> p h t", h=8),
                            in0=s_sb[:, hsl].rearrange("p (h t) -> p h t", h=8),
                            in1=ea_sb[:, c * NH + hb:c * NH + hb + 8]
                                .rearrange("p (h o) -> p h o", o=1).broadcast_to([P, 8, P]),
                            op=ALU.mult)
                        s_ve.inc(v.tensor_tensor(out=s_sb[:, hsl], in0=tsc2[:, :], in1=ps_d2[:, :], op=ALU.add))
                    p2["o_cons"] = s_ve.n

                @block.vector
                def _(v):
                    v.wait_ge(s_sc.h, er_tok)
                    v.tensor_copy(out=sbf_sb[:], in_=s_sb[:])
                    s_ve.inc(v.tensor_tensor(out=rn_sb[:, :], in0=rn_sb[:, :],
                                             in1=a_sb[:, c * NH:(c + 1) * NH], op=ALU.add))
                p2["o_cons"] = s_ve.n

            @block.vector
            def _(v):
                s_fr.inc(v.tensor_copy(out=tsc[:, 0:1], in_=tsc[:, 0:1]))
            p2["prev_tend"] = s_ve.n
            if i % 4 == 3:
                block.flush()

        @block.vector
        def _(v):
            v.wait_ge(s_fr.h, NT)
            v.tensor_copy(out=pubst[:, 0:D], in_=s_sb[:, :])
            s_ve.inc(v.tensor_copy(out=pubst[:, D:D + NH], in_=rn_sb[:, :]))
        pub_tok = s_ve.n

        @block.sync
        def _(sync):
            sync.wait_ge(s_ve.h, pub_tok)
            s_st.inc(sync.dma_start(out=pub_in[:, :], in_=pubst[:, :]), 16)
        pubst_tok = s_st.n

        @block.gpsimd
        def _(g):
            g.wait_ge(s_st.h, pubst_tok)
            s_cc.inc(g.collective_compute("AllGather", ALU.bypass, replica_groups=RG,
                                          ins=[pub_in.opt()], outs=[pub_out.opt()]))
        pubag_tok = s_cc.n

        join()

        block.flush()
        if PHASE_LIMIT < 5:
            finish()
            return nc
        esW.close()
        esB.close()
        esC = es.enter_context(contextlib.ExitStack())
        xqt = [sbx(esC, f"s_xqt{t}", [P, D], BF16) for t in range(4)]
        hsb = [sbx(esC, f"s_hsb{t}", [P, DH], F32) for t in range(4)]
        w1s = [sbx(esC, f"s_w1s{b}", [P, 16 * 512], BF16) for b in range(2)]
        w3s = [sbx(esC, f"s_w3s{b}", [P, 16 * 512], BF16) for b in range(2)]
        h1sl = sbx(esC, "s_h1sl", [P, 512], F32)
        hqs = [sbx(esC, f"s_hqs{i}", [P, 512], F32) for i in range(2)]
        hq_bf = sbx(esC, "s_hq_bf", [P, DH], BF16)
        hqf_st = sbx(esC, "s_hqf_st", [P, DH], BF16)
        hmx = sbx(esC, "s_hmx", [P, 1], F32)
        chv = sbx(esC, "s_chv", [P, 1], F32)

        @block.tensor
        def _(t):
            t.wait_ge(s_cc.h, wag_tok)

        g_w13 = [Guard(1), Guard(1)]
        g_y1 = [Guard(1), Guard(1)]
        g_y3 = [Guard(1), Guard(1)]
        g_h1sl = Guard(1)
        g_hqm = [Guard(1), Guard(1)]
        g_chv = Guard(1)
        g_hqbf = Guard(1)
        g_tp4 = Guard(1)
        g_hqfst = Guard(1)
        g_xqt = Guard(1)
        for st_i in range(P4_NST):
            tt0 = st_i * 4

            @block.sync
            def _(sync):
                sync.wait_ge(s_mm.h, g_xqt.tok())
                for tl in range(4):
                    r0 = (tt0 + tl) * P
                    s_xq4.inc(sync.dma_start(out=xqt[tl][:, :], in_=d_xqf[r0:r0 + P, :]), 16)
            xq_tok = s_xq4.n

            hmm_last = 0
            for hs in range(11):
                bw = hs % 2
                h0 = hs * 512

                @block.sync
                def _(sync):
                    sync.wait_ge(s_mm.h, g_w13[bw].tok())
                    for dc in range(16):
                        s_w13[bw].inc(sync.dma_start(
                            out=w1s[bw][:, dc * 512:(dc + 1) * 512],
                            in_=wag1_out[dc * P:(dc + 1) * P, h0:h0 + 512]), 16)
                        s_w13[bw].inc(sync.dma_start(
                            out=w3s[bw][:, dc * 512:(dc + 1) * 512],
                            in_=wag3_out[dc * P:(dc + 1) * P, h0:h0 + 512]), 16)
                w_sem, w_tok = s_w13[bw], s_w13[bw].n

                for tl in range(4):
                    gy = tl % 2
                    ppy1 = [ps_a, ps_b][gy]
                    ppy3v = [ps_c[:, :], ps_d2[:, 0:512]][gy]
                    ttg = tt0 + tl

                    @block.tensor
                    def _(t):
                        t.wait_ge(w_sem.h, w_tok)
                        t.wait_ge(s_xq4.h, xq_tok)
                        t.wait_ge(s_sc.h, g_y1[gy].tok())
                        t.wait_ge(s_ve.h, g_y3[gy].tok())
                        for dc in range(16):
                            t.matmul(ppy1[:, :], xqt[tl][:, dc * P:(dc + 1) * P],
                                     w1s[bw][:, dc * 512:(dc + 1) * 512],
                                     start=(dc == 0), stop=(dc == 15))
                        ins = None
                        for dc in range(16):
                            ins = t.matmul(ppy3v, xqt[tl][:, dc * P:(dc + 1) * P],
                                           w3s[bw][:, dc * 512:(dc + 1) * 512],
                                           start=(dc == 0), stop=(dc == 15))
                        s_mm.inc(ins)
                    y_tok = s_mm.n
                    hmm_last = y_tok

                    if not P4_NOEV:
                        @block.scalar
                        def _(sc):
                            sc.wait_ge(s_mm.h, y_tok)
                            if st_i == 0 and hs == 0 and tl == 0:
                                sc.wait_ge(s_ve.h, rsff_tok)
                            sc.wait_ge(s_ve.h, g_h1sl.tok())
                            scl = rs1_all[:, ttg:ttg + 1] if P4_EVMODE >= 3 else 1.0
                            s_sc.inc(sc.activation(out=h1sl[:, :], in_=ppy1[:, :], func=AF.Silu,
                                                   scale=scl))
                        g_y1[gy].push(s_sc.n)
                        h1_tok = s_sc.n

                        if P4_EVMODE >= 2:
                            @block.vector
                            def _(v):
                                v.wait_ge(s_sc.h, h1_tok)
                                scl2 = rs3_all[:, ttg:ttg + 1] if P4_EVMODE >= 3 else 1.0
                                s_ve.inc(v.scalar_tensor_tensor(
                                    out=hsb[tl][:, h0:h0 + 512], in0=ppy3v,
                                    scalar=scl2, in1=h1sl[:, :],
                                    op0=ALU.mult, op1=ALU.mult))
                            g_y3[gy].push(s_ve.n)
                            g_h1sl.push(s_ve.n)
                g_w13[bw].push(hmm_last)
            hdone_tok = s_ve.n
            g_xqt.push(hmm_last)

            for tl in range(4 if not P4_NOHQ else 0):
                ttg = tt0 + tl

                @block.vector
                def _(v):
                    v.wait_ge(s_ve.h, hdone_tok)
                    v.wait_ge(s_sc.h, g_chv.tok())
                    v.tensor_reduce(out=hmx[:, 0:1], in_=hsb[tl][:, :], axis=AX.X, op=ALU.max,
                                    apply_absolute_value=True).then_inc(s_ve.h, 1)
                    s_ve.n += 1
                    v.wait_ge(s_ve.h, s_ve.n)
                    v.tensor_scalar(out=hmx[:, 0:1], in0=hmx[:, 0:1], scalar1=float(Q_EPS),
                                    scalar2=None, op0=ALU.max).then_inc(s_ve.h, 1)
                    s_ve.n += 1
                    v.wait_ge(s_ve.h, s_ve.n)
                    v.reciprocal(out=chv[:, 0:1], in_=hmx[:, 0:1]).then_inc(s_ve.h, 1)
                    s_ve.n += 1
                    v.wait_ge(s_ve.h, s_ve.n)
                    v.tensor_scalar(out=chv[:, 0:1], in0=chv[:, 0:1], scalar1=127.0,
                                    scalar2=None, op0=ALU.mult).then_inc(s_ve.h, 1)
                    s_ve.n += 1
                    v.wait_ge(s_ve.h, s_ve.n)
                    s_ve.inc(v.tensor_tensor(out=rs2_all[:, ttg:ttg + 1], in0=hmx[:, 0:1],
                                             in1=wscl[:, 3:4], op=ALU.mult))
                hq1_tok = s_ve.n

                for hs2 in range(11):
                    bq = hs2 % 2
                    hsl2 = slice(hs2 * 512, (hs2 + 1) * 512)

                    @block.scalar
                    def _(sc):
                        if hs2 == 0:
                            sc.wait_ge(s_ve.h, hq1_tok)
                            sc.wait_ge(s_mm.h, g_tp4.hist[0])
                        sc.wait_ge(s_ve.h, g_hqm[bq].tok())
                        s_sc.inc(sc.activation(out=hqs[bq][:, :], in_=hsb[tl][:, hsl2], func=AF.Identity,
                                               scale=chv[:, 0:1], bias=np.float32(MAGIC)))
                    hqm_tok = s_sc.n

                    @block.vector
                    def _(v):
                        v.wait_ge(s_sc.h, hqm_tok)
                        s_ve.inc(v.tensor_scalar(out=hq_bf[:, hsl2], in0=hqs[bq][:, :], scalar1=MAGIC,
                                                 scalar2=None, op0=ALU.subtract))
                    g_hqm[bq].push(s_ve.n)
                g_chv.push(s_sc.n)
                hqb_tok = s_ve.n

                last_tp = 0
                for r in range(6):
                    n_in_r = 8 if r < 5 else 4

                    @block.tensor
                    def _(t):
                        t.wait_ge(s_ve.h, hqb_tok)
                        t.wait_ge(s_ve.h, state.get("p4tp_cons", 0))
                        ins = None
                        for j in range(n_in_r):
                            hc = r * 8 + j
                            ins = t.transpose(ps_tp[:, j * P:(j + 1) * P],
                                              hq_bf[:, hc * P:(hc + 1) * P], ident[:, :])
                        s_mm.inc(ins)
                    tp_tok = s_mm.n
                    last_tp = tp_tok

                    @block.vector
                    def _(v):
                        v.wait_ge(s_mm.h, tp_tok)
                        if r == 0:
                            v.wait_ge(s_hqst.h, g_hqfst.tok())
                        s_ve.inc(v.tensor_copy(out=hqf_st[:, r * 1024:r * 1024 + n_in_r * P],
                                               in_=ps_tp[:, 0:n_in_r * P]))
                    state["p4tp_cons"] = s_ve.n
                g_tp4.push(last_tp)
                hqf_tok = s_ve.n

                @block.sync
                def _(sync):
                    sync.wait_ge(s_ve.h, hqf_tok)
                    r0 = ttg * P
                    s_hqst.inc(sync.dma_start(out=d_hq[r0:r0 + P, :], in_=hqf_st[:, :]), 16)
                g_hqfst.push(s_hqst.n)

        join()

        block.flush()
        if PHASE_LIMIT < 6:
            finish()
            return nc
        esC.close()
        esD = es.enter_context(contextlib.ExitStack())
        w2s = [sbx(esD, f"s_w2s{d}", [P, 44 * 512], BF16) for d in range(4)]
        hqin = [sbx(esD, f"s_hqin{b}", [P, DH], BF16) for b in range(2)]
        gt_st = [sbx(esD, f"s_gt_st{b}", [P, 512], BF16) for b in range(2)]

        w2_toks = []

        @block.sync
        def _(sync):
            for do in range(4):
                o0 = do * 512
                s_w2l.inc(sync.dma_start(
                    out=w2s[do][:, :].rearrange("p (hc n) -> p hc n", hc=44),
                    in_=wag2_out[:, o0:o0 + 512].rearrange("(hc p) n -> p hc n", p=P)), 16)
                w2_toks.append(s_w2l.n)

        g_hqin = [Guard(1), Guard(1)]
        g_py = [Guard(1), Guard(1)]
        g_gt = [Guard(1), Guard(1)]
        for tt in range(NT):
            bh = tt % 2
            r0 = tt * P

            @block.sync
            def _(sync):
                sync.wait_ge(s_mm.h, g_hqin[bh].tok())
                s_hql01[bh].inc(sync.dma_start(out=hqin[bh][:, :], in_=d_hq[r0:r0 + P, :]), 16)
            hq_tok = s_hql01[bh].n

            last_mm = 0
            for do in range(4):
                o0 = do * 512
                bp = do % 2
                ppy = [ps_a, ps_b][bp]

                @block.tensor
                def _(t):
                    t.wait_ge(s_w2l.h, w2_toks[do])
                    if do == 0:
                        t.wait_ge(s_hql01[bh].h, hq_tok)
                    t.wait_ge(s_sc.h, g_py[bp].tok())
                    ins = None
                    for hc in range(44):
                        ins = t.matmul(ppy[:, :], hqin[bh][:, hc * P:(hc + 1) * P],
                                       w2s[do][:, hc * 512:(hc + 1) * 512],
                                       start=(hc == 0), stop=(hc == 43))
                    s_mm.inc(ins)
                y2_tok = s_mm.n
                last_mm = y2_tok

                @block.scalar
                def _(sc):
                    sc.wait_ge(s_mm.h, y2_tok)
                    sc.wait_ge(s_gt01[bp].h, g_gt[bp].tok())
                    s_sc.inc(sc.activation(out=gt_st[bp][:, :], in_=ppy[:, :], func=AF.Silu,
                                           scale=rs2_all[:, tt:tt + 1]))
                g_py[bp].push(s_sc.n)
                gt_tok = s_sc.n

                @block.sync
                def _(sync):
                    sync.wait_ge(s_sc.h, gt_tok)
                    s_gt01[bp].inc(sync.dma_start(out=d_gate[r0:r0 + P, o0:o0 + 512], in_=gt_st[bp][:, :]), 16)
                g_gt[bp].push(s_gt01[bp].n)
            g_hqin[bh].push(last_mm)

        join()

        block.flush()
        if PHASE_LIMIT < 7:
            finish()
            return nc
        esD.close()
        esE = es.enter_context(contextlib.ExitStack())
        wo_sb = sbx(esE, "s_wo_sb", [P, NT * D], BF16)
        wost = [sbx(esE, f"s_wost{b}", [P, D], F32) for b in range(2)]
        sinit = sbx(esE, "s_sinit", [P, D], F32)
        sinit_bf = sbx(esE, "s_sinit_bf", [P, D], BF16)
        uj = sbx(esE, "s_uj", [P, 2064], F32)
        bneg = sbx(esE, "s_bneg", [P, NH], F32)
        ejv = sbx(esE, "s_ejv", [P, NH], F32)
        tsc2 = sbx(esE, "s_tsc2e", [P, 1024], F32)
        ol_in = [sbx(esE, f"s_ol_in{b}", [P, 2048], BF16) for b in range(2)]
        qh_in = [sbx(esE, f"s_qh_in{b}", [P, D], BF16) for b in range(2)]
        gt_in = [sbx(esE, f"s_gt_in{b}", [P, D], BF16) for b in range(2)]
        x6 = [sbx(esE, f"s_x6{b}", [P, D], F32) for b in range(2)]
        of_sb = sbx(esE, "s_of_sb", [P, D], F32)
        o2_sb = sbx(esE, "s_o2_sb", [P, D], BF16)
        ob_sb = sbx(esE, "s_ob_sb", [P, D], BF16)
        rr_sb = sbx(esE, "s_rr_sb", [P, 1], F32)
        ss_sb = sbx(esE, "s_ss_sb", [P, 1], F32)
        sv_sb = sbx(esE, "s_sv_sb", [P, 512], F32)
        sg_sb = sbx(esE, "s_sg_sb", [P, 512], F32)
        out_sb = sbx(esE, "s_out_sb", [P, D], F32)
        g_wost = [Guard(1), Guard(1)]
        for dc in range(16):
            bw2 = dc % 2

            @block.sync
            def _(sync):
                sync.wait_ge(s_ve.h, g_wost[bw2].tok())
                s_wol01[bw2].inc(sync.dma_start(out=wost[bw2][:, :], in_=ap_wo[dc * P:(dc + 1) * P, :]), 16)
            wol_tok = s_wol01[bw2].n

            @block.vector
            def _(v):
                v.wait_ge(s_wol01[bw2].h, wol_tok)
                s_ve.inc(v.tensor_copy(out=wo_sb[:, dc * D:(dc + 1) * D], in_=wost[bw2][:, :]))
            g_wost[bw2].push(s_ve.n)
        wo_tok = s_ve.n

        @block.vector
        def _(v):
            v.memset(sinit[:, :], 0.0)
            s_ve.inc(v.memset(bneg[:, :], 0.0))
        si_prev = s_ve.n

        for j in range(NCORE - 1, -1, -1):
            @block.sync
            def _(sync):
                sync.wait_ge(s_cc.h, pubag_tok)
                sync.wait_ge(s_ve.h, si_prev)
                s_uj.inc(sync.dma_start(out=uj[:, :], in_=pub_out[j * P:(j + 1) * P, :]), 16)
            uj_tok = s_uj.n

            @block.scalar
            def _(sc):
                sc.wait_ge(s_ve.h, si_prev)
                s_sc.inc(sc.activation(out=ejv[:, :], in_=bneg[:, :], func=AF.Exp, scale=-1.0))
            ej_tok = s_sc.n

            @block.vector
            def _(v):
                v.wait_ge(s_sc.h, ej_tok)
                v.wait_ge(s_uj.h, uj_tok)
                v.wait_ge(s_ld.h, const_tok)
                v.tensor_tensor(out=ejv[:, :], in0=ejv[:, :],
                                in1=cmsk[:, j:j + 1].broadcast_to([P, NH]), op=ALU.mult)
                v.tensor_tensor(out=tsc2[:, 0:1024].rearrange("p (h t) -> p h t", h=8),
                                in0=uj[:, 0:1024].rearrange("p (h t) -> p h t", h=8),
                                in1=ejv[:, 0:8].rearrange("p (h o) -> p h o", o=1).broadcast_to([P, 8, P]),
                                op=ALU.mult)
                v.tensor_tensor(out=sinit[:, 0:1024], in0=sinit[:, 0:1024], in1=tsc2[:, 0:1024], op=ALU.add)
                v.tensor_tensor(out=tsc2[:, 0:1024].rearrange("p (h t) -> p h t", h=8),
                                in0=uj[:, 1024:2048].rearrange("p (h t) -> p h t", h=8),
                                in1=ejv[:, 8:16].rearrange("p (h o) -> p h o", o=1).broadcast_to([P, 8, P]),
                                op=ALU.mult)
                v.tensor_tensor(out=sinit[:, 1024:2048], in0=sinit[:, 1024:2048], in1=tsc2[:, 0:1024], op=ALU.add)
                v.tensor_tensor(out=ejv[:, :], in0=uj[:, D:D + NH],
                                in1=cmsk[:, j:j + 1].broadcast_to([P, NH]), op=ALU.mult)
                s_ve.inc(v.tensor_tensor(out=bneg[:, :], in0=bneg[:, :], in1=ejv[:, :], op=ALU.add))
            si_prev = s_ve.n

        @block.vector
        def _(v):
            s_ve.inc(v.tensor_copy(out=sinit_bf[:, :], in_=sinit[:, :]))
        sib_tok = s_ve.n

        g_p6in = [Guard(1), Guard(1)]
        g_corr = Guard(1)
        g_of = Guard(1)
        g_ob = Guard(1)
        g_ssq = Guard(1)
        g_ps6 = [Guard(1), Guard(1)]
        g_out = Guard(1)
        for tt in range(NT):
            b = tt % 2
            t0 = tt * P

            @block.sync
            def _(sync):
                sync.wait_ge(s_ve.h, g_p6in[b].tok())
                s_p6l01[b].inc(sync.dma_start(out=ol_in[b][:, 0:1024], in_=d_ol[(2 * tt) * P:(2 * tt + 1) * P, :]), 16)
                s_p6l01[b].inc(sync.dma_start(out=ol_in[b][:, 1024:2048], in_=d_ol[(2 * tt + 1) * P:(2 * tt + 2) * P, :]), 16)
                s_p6l01[b].inc(sync.dma_start(out=qh_in[b][:, :], in_=d_qh[:, tt * D:(tt + 1) * D]), 16)
                s_p6l01[b].inc(sync.dma_start(out=gt_in[b][:, :], in_=d_gate[t0:t0 + P, :]), 16)
                s_p6l01[b].inc(sync.dma_start(out=x6[b][:, :], in_=ap_x[t0:t0 + P, :]), 16)
            p6l_tok = s_p6l01[b].n

            @block.tensor
            def _(t):
                t.wait_ge(s_p6l01[b].h, p6l_tok)
                t.wait_ge(s_ve.h, sib_tok)
                t.wait_ge(s_ve.h, g_corr.tok())
                ins = None
                for h in range(NH):
                    pps6 = ps_d2 if h < 8 else ps_e2
                    hb6 = (h % 8) * P
                    for cc_ in range(2):
                        ins = t.matmul(pps6[:, hb6 + cc_ * C:hb6 + cc_ * C + C],
                                       sinit_bf[:, h * P:(h + 1) * P],
                                       qh_in[b][:, h * P + cc_ * C:h * P + cc_ * C + C],
                                       start=True, stop=True)
                s_mm.inc(ins)
            corr_tok = s_mm.n

            @block.vector
            def _(v):
                v.wait_ge(s_mm.h, corr_tok)
                v.wait_ge(s_sc.h, g_of.tok())
                v.wait_ge(s_mm.h, g_ob.tok())
                olv = ol_in[b][:, :].rearrange("p (c h q) -> p h c q", c=2, h=NH)
                v.tensor_tensor(
                    out=of_sb[:, 0:1024].rearrange("p (h c q) -> p h c q", h=8, c=2),
                    in0=ps_d2[:, :].rearrange("p (h c q) -> p h c q", h=8, c=2),
                    in1=olv[:, 0:8],
                    op=ALU.add)
                v.tensor_tensor(
                    out=of_sb[:, 1024:2048].rearrange("p (h c q) -> p h c q", h=8, c=2),
                    in0=ps_e2[:, :].rearrange("p (h c q) -> p h c q", h=8, c=2),
                    in1=olv[:, 8:16],
                    op=ALU.add)
                s_ve.inc(v.tensor_tensor(
                    out=ob_sb[:, :].rearrange("p (h t) -> p h t", h=NH),
                    in0=of_sb[:, :].rearrange("p (h t) -> p h t", h=NH),
                    in1=gw_fm[:, :].rearrange("p (h o) -> p h o", o=1).broadcast_to([P, NH, P]),
                    op=ALU.mult))
            of_tok = s_ve.n
            g_corr.push(of_tok)

            @block.scalar
            def _(sc):
                sc.wait_ge(s_ve.h, of_tok)
                s_sc.inc(sc.activation(out=o2_sb[:, :], in_=of_sb[:, :], func=AF.Square))
            o2_tok = s_sc.n
            g_of.push(o2_tok)

            @block.tensor
            def _(t):
                t.wait_ge(s_sc.h, o2_tok)
                t.wait_ge(s_ve.h, g_ssq.tok())
                ins = None
                for h in range(NH):
                    ins = t.matmul(ps_c[:, 0:1], o2_sb[:, h * P:(h + 1) * P], ones1[:, 0:1],
                                   start=(h == 0), stop=(h == NH - 1))
                s_mm.inc(ins)
            ssq_tok = s_mm.n

            @block.scalar
            def _(sc):
                sc.wait_ge(s_mm.h, ssq_tok)
                s_sc.inc(sc.activation(out=ss_sb[:, 0:1], in_=ps_c[:, 0:1], func=AF.Sqrt,
                                       scale=float(1.0 / D), bias=np.float32(LN_EPS)))
            sstd_tok = s_sc.n

            @block.vector
            def _(v):
                v.wait_ge(s_sc.h, sstd_tok)
                v.reciprocal(out=rr_sb[:, 0:1], in_=ss_sb[:, 0:1]).then_inc(s_ve.h, 1)
                s_ve.n += 1
                v.wait_ge(s_ve.h, s_ve.n)
                s_ve.inc(v.tensor_copy(out=st1[:, 0:1], in_=rr_sb[:, 0:1]))
            rr_tok = s_ve.n
            g_ssq.push(rr_tok)

            last_wo_mm = 0
            for do in range(4):
                o0 = do * 512
                gdo = do % 2
                ppy = [ps_a, ps_b][gdo]

                @block.tensor
                def _(t):
                    t.wait_ge(s_ve.h, of_tok)
                    t.wait_ge(s_ve.h, wo_tok)
                    t.wait_ge(s_ve.h, g_ps6[gdo].tok())
                    ins = None
                    for h in range(NH):
                        ins = t.matmul(ppy[:, :], ob_sb[:, h * P:(h + 1) * P],
                                       wo_sb[:, h * D + o0:h * D + o0 + 512],
                                       start=(h == 0), stop=(h == NH - 1))
                    s_mm.inc(ins)
                smm_tok = s_mm.n
                last_wo_mm = smm_tok

                @block.vector
                def _(v):
                    v.wait_ge(s_mm.h, smm_tok)
                    v.wait_ge(s_ve.h, rr_tok)
                    if do == 0:
                        v.wait_ge(s_y.h, g_out.tok())
                    v.tensor_scalar(out=sv_sb[:, :], in0=ppy[:, :], scalar1=rr_sb[:, 0:1],
                                    scalar2=None, op0=ALU.mult)
                    v.tensor_tensor(out=sg_sb[:, :], in0=sv_sb[:, :],
                                    in1=gt_in[b][:, o0:o0 + 512], op=ALU.mult)
                    s_ve.inc(v.tensor_tensor(out=out_sb[:, o0:o0 + 512], in0=sg_sb[:, :],
                                             in1=x6[b][:, o0:o0 + 512], op=ALU.add))
                g_ps6[gdo].push(s_ve.n)
            odone_tok = s_ve.n
            g_ob.push(last_wo_mm)
            g_p6in[b].push(odone_tok)

            @block.sync
            def _(sync):
                sync.wait_ge(s_ve.h, odone_tok)
                s_y.inc(sync.dma_start(out=ap_y[t0:t0 + P, :], in_=out_sb[:, :]), 16)
            g_out.push(s_y.n)
            if tt % 4 == 3:
                block.flush()

        if KDBG:
            srcs = dict(ar_out=ar_out, d_sph=d_sph, d_d=d_d, d_v=d_v, d_q=d_q,
                        d_xqf=d_xqf, d_qh=d_qh, d_ol=d_ol, d_hq=d_hq, d_gate=d_gate,
                        pub_out=pub_out, wag1_out=wag1_out, d_scl=d_scl)
            srcs["d_stats"] = d_stats
            srcs["ar_in"] = ar_in
            srcs["d_warb"] = d_warb
            srcs["d_wsl"] = d_wsl
            dbg.update(dbg2)

            @block.sync
            def _(sync):
                sync.wait_ge(s_st.h, s_st.n)
                sync.wait_ge(s_ve.h, s_ve.n)
                sync.wait_ge(s_sc.h, s_sc.n)
                for nm, dst in dbg.items():
                    s_st.inc(sync.dma_start(out=dst[:, :], in_=srcs[nm][:, :]), 16)

        @block.sync
        def _(sync):
            sync.wait_ge(s_st.h, s_st.n)
            sync.wait_ge(s_y.h, s_y.n)
        block.flush()
        esE.close()

    return nc


state = {}
wq3_toks = []


def state_get_wq3(idx):
    return wq3_toks[idx - 1] if idx - 1 >= 0 and idx - 1 < len(wq3_toks) else 0


_NC_CACHE = [None]


def kernel(L0_attn_hidden_states, L0_attn_Wq, L0_attn_Wk, L0_attn_Wi, L0_attn_Wo,
           L0_attn_g_weight, L0_ffn_w1, L0_ffn_w2, L0_ffn_w3):
    x = np.ascontiguousarray(np.asarray(L0_attn_hidden_states, np.float32).reshape(16384, D))
    wq = np.ascontiguousarray(np.asarray(L0_attn_Wq, np.float32))
    wk = np.ascontiguousarray(np.asarray(L0_attn_Wk, np.float32))
    wi = np.ascontiguousarray(np.asarray(L0_attn_Wi, np.float32))
    wo = np.ascontiguousarray(np.asarray(L0_attn_Wo, np.float32))
    w1t = np.ascontiguousarray(np.asarray(L0_ffn_w1, np.float32).T)
    w3t = np.ascontiguousarray(np.asarray(L0_ffn_w3, np.float32).T)
    w2t = np.ascontiguousarray(np.asarray(L0_ffn_w2, np.float32).T)
    gwv = np.ascontiguousarray(np.asarray(L0_attn_g_weight, np.float32).reshape(1, D))

    tri_i = np.zeros((P, P), np.float32)
    tri_x = np.zeros((P, P), np.float32)
    io, jo = np.meshgrid(np.arange(C), np.arange(C), indexing="ij")
    for blk in range(2):
        s0 = blk * C
        tri_i[s0:s0 + C, s0:s0 + C] = (io <= jo)
        tri_x[s0:s0 + C, s0:s0 + C] = (io > jo)
    maskp = np.zeros((P, 1024), np.float32)
    for pp in range(P):
        kt = pp % C
        maskp[pp, :] = np.tile((kt <= np.arange(C)).astype(np.float32), NH)
    identm = np.eye(P, dtype=np.float32)
    cselh = np.zeros((P, 2), np.float32)
    cselh[0:C, 0] = 1
    cselh[C:P, 1] = 1
    onesh = np.ones((P, 1), np.float32)

    bfc = lambda a: a.astype(ml_dtypes.bfloat16)

    if _NC_CACHE[0] is None:
        state.clear()
        del wq3_toks[:]
        _NC_CACHE[0] = build_program()
    nc = _NC_CACHE[0]

    in_maps = []
    for c in range(NCORE):
        cmask = np.zeros((1, 8), np.float32)
        cmask[0, :c] = 1.0
        in_maps.append({
            "x": x[c * T:(c + 1) * T],
            "wq": wq, "wk": wk, "wi": wi, "wo": wo,
            "gwv": gwv, "cmask": cmask,
            "ctri_i": bfc(tri_i), "ctri_x": bfc(tri_x), "cident": bfc(identm),
            "cmaskp": maskp, "ccsel": bfc(cselh), "cones": bfc(onesh),
            "w1own": np.ascontiguousarray(w1t[c * (D // NCORE):(c + 1) * (D // NCORE)]),
            "w3own": np.ascontiguousarray(w3t[c * (D // NCORE):(c + 1) * (D // NCORE)]),
            "w2own": np.ascontiguousarray(w2t[c * W2ROWS:(c + 1) * W2ROWS]),
        })
    res = run_bass_kernel_spmd(nc, in_maps, core_ids=list(range(NCORE)))
    y = np.concatenate([np.asarray(res.results[c]["y"]) for c in range(NCORE)], axis=0)
    import os as _os
    if _os.environ.get("KERN_DEBUG") == "1":
        kernel.debug = [{k: np.asarray(v) for k, v in res.results[c].items()} for c in range(NCORE)]
    return y.reshape(1, 16384, D)


# version 07
# speedup vs baseline: 1.0401x; 1.0401x over previous
import numpy as np
import ml_dtypes
import contextlib
import concourse.bass as bass
import concourse.mybir as mybir
from concourse.bass_utils import run_bass_kernel_spmd

F32 = mybir.dt.float32
BF16 = mybir.dt.bfloat16
AF = mybir.ActivationFunctionType
ALU = mybir.AluOpType
AX = mybir.AxisListType

P = 128
C = 64
NH = 16
D = 2048
DH = 5632
T = 2048
NT = T // P
NCH = T // C
NCORE = 8
MAGIC = float(3 * 2.0 ** 22)
C0 = float(np.log(np.sqrt(128.0)))
LN_EPS = 1e-5
Q_EPS = 1e-5
W2ROWS = DH // NCORE


class Sem:
    def __init__(self, h):
        self.h = h
        self.n = 0

    def inc(self, instr, k=1):
        instr.then_inc(self.h, k)
        self.n += k
        return self.n


class Guard:

    def __init__(self, depth=2):
        self.hist = [0] * depth

    def tok(self):
        return self.hist[0]

    def push(self, t):
        self.hist = self.hist[1:] + [t]


import os
PHASE_LIMIT = int(os.environ.get("KERN_PHASES", "9"))
P4_NOHQ = bool(int(os.environ.get("KERN_P4_NOHQ", "0")))
P4_NST = int(os.environ.get("KERN_P4_NST", "4"))
P4_NOEV = bool(int(os.environ.get("KERN_P4_NOEV", "0")))
KDBG = bool(int(os.environ.get("KERN_DEBUG", "0")))
P4_EVMODE = int(os.environ.get("KERN_P4_EVMODE", "3"))


class RecInstr:
    __slots__ = ("incs",)

    def __init__(self):
        self.incs = []

    def then_inc(self, sem, k=1):
        self.incs.append((sem, k))
        return self


class RecEng:
    def __init__(self, rec, key):
        self._rec = rec
        self._key = key

    def __getattr__(self, name):
        rec, key = self._rec, self._key

        def call(*a, **kw):
            ri = RecInstr()
            rec.ops[key].append((name, a, kw, ri))
            return ri
        return call


class Recorder:
    KEYS = ("sync", "scalar", "vector", "tensor", "gpsimd")

    def __init__(self, nc):
        self.nc = nc
        self.ops = {k: [] for k in self.KEYS}

    def _section(self, key):
        def deco(f):
            f(RecEng(self, key))
        return deco

    @property
    def sync(self):
        return self._section("sync")

    @property
    def scalar(self):
        return self._section("scalar")

    @property
    def vector(self):
        return self._section("vector")

    @property
    def tensor(self):
        return self._section("tensor")

    @property
    def gpsimd(self):
        return self._section("gpsimd")

    def flush(self):
        pend = {k: v for k, v in self.ops.items() if v}
        if not pend:
            return
        self.ops = {k: [] for k in self.KEYS}
        with self.nc.Block() as blk:
            for key, ops in pend.items():
                def mk(ops):
                    def run(eng):
                        for (name, a, kw, ri) in ops:
                            instr = getattr(eng, name)(*a, **kw)
                            for (sem, k) in ri.incs:
                                instr.then_inc(sem, k)
                    return run
                getattr(blk, key)(mk(ops))


def build_program():
    nc = bass.Bass("TRN2", target_bir_lowering=False, debug=False)

    ap_x = nc.dram_tensor("x", [T, D], F32, kind="ExternalInput").ap()
    ap_wq = nc.dram_tensor("wq", [D, D], F32, kind="ExternalInput").ap()
    ap_wk = nc.dram_tensor("wk", [D, D], F32, kind="ExternalInput").ap()
    ap_wi = nc.dram_tensor("wi", [D, D], F32, kind="ExternalInput").ap()
    ap_wo = nc.dram_tensor("wo", [D, D], F32, kind="ExternalInput").ap()
    ap_w1own = nc.dram_tensor("w1own", [D // NCORE, DH], F32, kind="ExternalInput").ap()
    ap_w3own = nc.dram_tensor("w3own", [D // NCORE, DH], F32, kind="ExternalInput").ap()
    ap_w2own = nc.dram_tensor("w2own", [W2ROWS, D], F32, kind="ExternalInput").ap()
    ap_gw = nc.dram_tensor("gwv", [1, D], F32, kind="ExternalInput").ap()
    ap_cmask = nc.dram_tensor("cmask", [1, 8], F32, kind="ExternalInput").ap()
    ap_tri_i = nc.dram_tensor("ctri_i", [P, P], BF16, kind="ExternalInput").ap()
    ap_tri_x = nc.dram_tensor("ctri_x", [P, P], BF16, kind="ExternalInput").ap()
    ap_ident = nc.dram_tensor("cident", [P, P], BF16, kind="ExternalInput").ap()
    ap_maskp = nc.dram_tensor("cmaskp", [P, 1024], F32, kind="ExternalInput").ap()
    ap_csel = nc.dram_tensor("ccsel", [P, 2], BF16, kind="ExternalInput").ap()
    ap_ones = nc.dram_tensor("cones", [P, 1], BF16, kind="ExternalInput").ap()
    ap_y = nc.dram_tensor("y", [T, D], F32, kind="ExternalOutput").ap()

    d_sph = nc.dram_tensor("d_sph", [T, D], BF16).ap()
    d_spl = nc.dram_tensor("d_spl", [T, D], BF16).ap()
    d_d = nc.dram_tensor("d_d", [T, D], F32).ap()
    d_v = nc.dram_tensor("d_v", [T, D], BF16).ap()
    d_q = nc.dram_tensor("d_q", [T, D], BF16).ap()
    d_xqf = nc.dram_tensor("d_xqf", [T, D], BF16).ap()
    d_qh = nc.dram_tensor("d_qh", [P, NT * D], BF16).ap()
    d_ol = nc.dram_tensor("d_ol", [NCH * P, 1024], BF16).ap()
    d_hq = nc.dram_tensor("d_hq", [T, DH], BF16).ap()
    d_gate = nc.dram_tensor("d_gate", [T, D], BF16).ap()
    d_scl = nc.dram_tensor("d_scl", [1, 64], F32).ap()

    ar_in = nc.dram_tensor("ar_in", [1, 16], F32).ap()
    ar_out = nc.dram_tensor("ar_out", [1, 16], F32).ap()
    wag1_in = nc.dram_tensor("wag1_in", [D // NCORE, DH], BF16).ap()
    wag1_out = nc.dram_tensor("wag1_out", [D, DH], BF16).ap()
    wag3_in = nc.dram_tensor("wag3_in", [D // NCORE, DH], BF16).ap()
    wag3_out = nc.dram_tensor("wag3_out", [D, DH], BF16).ap()
    wag2_in = nc.dram_tensor("wag2_in", [W2ROWS, D], BF16).ap()
    wag2_out = nc.dram_tensor("wag2_out", [DH, D], BF16).ap()
    pub_in = nc.dram_tensor("pub_in", [P, 2064], F32).ap()
    pub_out = nc.dram_tensor("pub_out", [NCORE * P, 2064], F32).ap()
    dbg = {}
    d_wsl = nc.dram_tensor("d_wsl", [4 * P, 16 * 512], BF16).ap()
    d_warb = nc.dram_tensor("d_warb", [P, 24], F32).ap()
    d_stats = nc.dram_tensor("d_stats", [P, 8 * NT], F32).ap()
    dbg2 = {}
    if KDBG:
        dbg2["d_stats"] = nc.dram_tensor("dbg_d_stats", [P, 8 * NT], F32, kind="ExternalOutput").ap()
    if KDBG:
        for nm, shp, dt in [("d_wsl", [4 * P, 16 * 512], BF16),
                            ("ar_out", [1, 16], F32), ("ar_in", [1, 16], F32),
                            ("d_warb", [P, 24], F32), ("d_sph", [T, D], BF16),
                            ("d_d", [T, D], F32), ("d_v", [T, D], BF16), ("d_q", [T, D], BF16),
                            ("d_xqf", [T, D], BF16), ("d_qh", [P, NT * D], BF16),
                            ("d_ol", [NCH * P, 1024], BF16), ("d_hq", [T, DH], BF16),
                            ("d_gate", [T, D], BF16), ("pub_out", [NCORE * P, 2064], F32),
                            ("wag1_out", [D, DH], BF16), ("d_scl", [1, 64], F32)]:
            dbg[nm] = nc.dram_tensor("dbg_" + nm, shp, dt, kind="ExternalOutput").ap()

    RG = [list(range(NCORE))]

    es = contextlib.ExitStack()
    with es:
        sb = lambda name, shape, dt=F32: es.enter_context(nc.sbuf_tensor(name, shape, dt))
        psr = lambda name, shape, dt=F32: es.enter_context(nc.psum_tensor(name, shape, dt))
        mks = lambda name: Sem(es.enter_context(nc.semaphore(name)))
        block = Recorder(nc)

        tri_i = sb("s_tri_i", [P, P], BF16)
        tri_x = sb("s_tri_x", [P, P], BF16)
        ident = sb("s_ident", [P, P], BF16)
        maskp = sb("s_maskp", [P, 1024], F32)
        csel = sb("s_csel", [P, 2], BF16)
        ones1 = sb("s_ones1", [P, 1], BF16)
        gw_fm = sb("s_gw_fm", [P, NH], F32)
        cmsk = sb("s_cmsk", [P, 8], F32)
        sx_all = sb("s_sx_all", [P, NT], F32)
        sxi_all = sb("s_sxi_all", [P, NT], F32)
        wpart = sb("s_wpart", [P, 16], F32)
        wsums = sb("s_wsums", [1, 16], F32)
        wscl = sb("s_wscl", [P, 8], F32)
        rs1_all = sb("s_rs1", [P, NT], F32)
        rs3_all = sb("s_rs3", [P, NT], F32)
        rs2_all = sb("s_rs2", [P, NT], F32)
        st1 = sb("s_st1", [P, 1], F32)
        st2 = sb("s_st2", [P, 1], F32)
        st3 = sb("s_st3", [P, 1], F32)
        st4 = sb("s_st4", [P, 1], F32)
        ones1f = sb("s_ones1f", [P, 1], F32)
        warb = sb("s_warb", [P, 16], F32)
        warb2 = sb("s_warb2", [P, 8], F32)
        stk = sb("s_stk", [P, 1], F32)

        ps_a = psr("ps_a", [P, 512], F32)
        ps_b = psr("ps_b", [P, 512], F32)
        ps_c = psr("ps_c", [P, 512], F32)
        ps_tp = psr("ps_tp", [P, 1024], BF16)
        ps_d2 = psr("ps_d2", [P, 1024], F32)
        ps_e2 = psr("ps_e2", [P, 1024], F32)

        s_ld = mks("m_ld")
        s_x01 = [mks("m_x0"), mks("m_x1")]
        s_wq01 = [mks("m_wq0"), mks("m_wq1")]
        s_wl01 = [mks("m_wl0"), mks("m_wl1"), mks("m_wl2"), mks("m_wl3")]
        s_pst01 = [mks("m_pst0"), mks("m_pst1")]
        s_pso01 = [mks("m_pso0"), mks("m_pso1")]
        s_p2l01 = [mks("m_p2l0"), mks("m_p2l1")]
        s_ost = mks("m_ost")
        s_qhst = mks("m_qhst")
        s_xqf = mks("m_xqf")
        s_xq4 = mks("m_xq4")
        s_w13 = [mks("m_w13a"), mks("m_w13b")]
        s_hqst = mks("m_hqst")
        s_w2l = mks("m_w2l")
        s_hql01 = [mks("m_hql0"), mks("m_hql1")]
        s_gt01 = [mks("m_gt0"), mks("m_gt1")]
        s_p6l01 = [mks("m_p6l0"), mks("m_p6l1")]
        s_wol01 = [mks("m_wol0"), mks("m_wol1")]
        s_uj = mks("m_uj")
        s_y = mks("m_y")
        s_mm = mks("m_mm")
        s_ve = mks("m_ve")
        s_sc = mks("m_sc")
        s_st = mks("m_st")
        s_cc = mks("m_cc")
        s_fr = mks("m_fr")

        def sbx(stk, name, shape, dt=F32):
            return stk.enter_context(nc.sbuf_tensor(name, shape, dt))

        def join():
            tv, ts, tm, tl, tst = s_ve.n, s_sc.n, s_mm.n, s_ld.n, s_st.n

            @block.vector
            def _(v):
                v.wait_ge(s_sc.h, ts)
                v.wait_ge(s_mm.h, tm)
                v.wait_ge(s_st.h, tst)

            @block.scalar
            def _(sc):
                sc.wait_ge(s_ve.h, tv)
                sc.wait_ge(s_mm.h, tm)

            @block.tensor
            def _(t):
                t.wait_ge(s_ve.h, tv)
                t.wait_ge(s_sc.h, ts)
                t.wait_ge(s_ld.h, tl)

            @block.sync
            def _(sync):
                sync.wait_ge(s_ve.h, tv)
                sync.wait_ge(s_sc.h, ts)
                sync.wait_ge(s_mm.h, tm)
                sync.wait_ge(s_st.h, tst)

        def finish():
            @block.sync
            def _(sync):
                sync.wait_ge(s_st.h, s_st.n)
                sync.wait_ge(s_y.h, s_y.n)
            block.flush()

        @block.sync
        def _(sync):
            s_ld.inc(sync.dma_start(out=tri_i[:], in_=ap_tri_i[:, :]), 16)
            s_ld.inc(sync.dma_start(out=tri_x[:], in_=ap_tri_x[:, :]), 16)
            s_ld.inc(sync.dma_start(out=ident[:], in_=ap_ident[:, :]), 16)
            s_ld.inc(sync.dma_start(out=maskp[:], in_=ap_maskp[:, :]), 16)
            s_ld.inc(sync.dma_start(out=csel[:], in_=ap_csel[:, :]), 16)
            s_ld.inc(sync.dma_start(out=ones1[:], in_=ap_ones[:, :]), 16)
            s_ld.inc(sync.dma_start(out=cmsk[:, :], in_=ap_cmask[0:1, 0:8].broadcast_to([P, 8])), 16)
            s_ld.inc(sync.dma_start(
                out=gw_fm[:, :], in_=ap_gw[0:1, :].rearrange("o (h p) -> (o p) h", h=NH),
                allow_slow_non_contiguous=True), 16)
        const_tok = s_ld.n

        esA = es.enter_context(contextlib.ExitStack())
        xn_fm = sbx(esA, "s_xn_fm", [P, NT * D], BF16)
        wsl = [sbx(esA, f"s_wsl{b}", [P, 16 * 512], BF16) for b in range(4)]
        wld = [sbx(esA, f"s_wld{b}", [P, 512], F32) for b in range(4)]
        ev1 = [sbx(esA, f"s_ev1{b}", [P, 512], F32) for b in range(2)]
        ev2 = [sbx(esA, f"s_ev2{b}", [P, 512], F32) for b in range(2)]
        evh = [sbx(esA, f"s_evh{b}", [P, 512], BF16) for b in range(2)]
        evl = [sbx(esA, f"s_evl{b}", [P, 512], BF16) for b in range(2)]
        evf = [sbx(esA, f"s_evf{b}", [P, 512], F32) for b in range(2)]
        evo = [sbx(esA, f"s_evo{b}", [P, 512], BF16) for b in range(2)]
        esA0 = es.enter_context(contextlib.ExitStack())
        xz = [sbx(esA0, f"s_xz{b}", [P, D], F32) for b in range(2)]
        xnz = sbx(esA0, "s_xnz", [P, D], F32)
        xqm = sbx(esA0, "s_xqm", [P, D], F32)
        xsq = xqm
        xq_bf = sbx(esA0, "s_xq_bf", [P, D], BF16)
        xn_bf = sbx(esA0, "s_xn_bf", [P, D], BF16)
        fmst = sbx(esA0, "s_fmst", [P, D], BF16)
        WQW = 2816

        if PHASE_LIMIT < 2:
            finish()
            return nc
        g_xz = [Guard(1), Guard(1)]
        g_tp = Guard(1)
        g_fmst = Guard(1)
        g_xbuf = Guard(1)
        cast_tok_k = [None] * 4
        next_k = [0, 0]
        g_kps = [Guard(1), Guard(1)]
        g_kcast = Guard(4)
        fm_toks = []
        wk_ld = []

        def rec_k_tile(tk, cs):
            c0 = cs * 512
            t0k = tk * P
            pps = [ps_a, ps_b][cs]
            be = cs

            @block.tensor
            def _(t):
                t.wait_ge(s_sc.h, cast_tok_k[cs])
                t.wait_ge(s_ve.h, fm_toks[tk])
                t.wait_ge(s_ve.h, g_kps[cs].tok())
                ins = None
                for dc in range(16):
                    ins = t.matmul(
                        pps[:, :],
                        xn_fm[:, :].rearrange("p (dc t) -> p dc t", dc=NT)[:, dc, t0k:t0k + P],
                        wsl[cs][:, dc * 512:(dc + 1) * 512],
                        start=(dc == 0), stop=(dc == 15))
                s_mm.inc(ins)
            mm_tok = s_mm.n

            @block.scalar
            def _(sc):
                sc.wait_ge(s_mm.h, mm_tok)
                sc.wait_ge(s_ve.h, state.get(f"p1dv_{be}", 0))
                sc.activation(out=ev1[be][:, :], in_=pps[:, :], func=AF.Exp)
                s_sc.inc(sc.activation(out=ev2[be][:, :], in_=ev1[be][:, :], func=AF.Ln,
                                       bias=np.float32(1.0)))
            sp_tok = s_sc.n

            @block.vector
            def _(v):
                v.wait_ge(s_sc.h, sp_tok)
                v.wait_ge(s_pst01[be].h, state.get(f"p1st_{be}", 0))
                v.tensor_copy(out=evh[be][:, :], in_=ev2[be][:, :])
                v.tensor_tensor(out=evf[be][:, :], in0=pps[:, :], in1=ev2[be][:, :],
                                op=ALU.subtract)
                s_ve.inc(v.tensor_tensor(out=evl[be][:, :], in0=ev2[be][:, :],
                                         in1=evh[be][:, :], op=ALU.subtract))
            state[f"p1dv_{be}"] = s_ve.n
            g_kps[cs].push(s_ve.n)

            @block.sync
            def _(sync):
                sync.wait_ge(s_ve.h, state[f"p1dv_{be}"])
                s_pst01[be].inc(sync.dma_start(out=d_d[t0k:t0k + P, c0:c0 + 512], in_=evf[be][:, :]), 16)
                s_pst01[be].inc(sync.dma_start(out=d_sph[t0k:t0k + P, c0:c0 + 512], in_=evh[be][:, :]), 16)
                s_pst01[be].inc(sync.dma_start(out=d_spl[t0k:t0k + P, c0:c0 + 512], in_=evl[be][:, :]), 16)
            state[f"p1st_{be}"] = s_pst01[be].n

        for tt in range(NT):
            b = tt % 2
            t0 = tt * P

            @block.sync
            def _(sync):
                sync.wait_ge(s_sc.h, g_xz[b].tok())
                s_x01[b].inc(sync.dma_start(out=xz[b][:], in_=ap_x[t0:t0 + P, :]), 16)
            xl_sem, xl_tok = s_x01[b], s_x01[b].n

            for j4 in range(4):
                ck = 4 * tt + j4
                cs_k, dc_k = ck // 16, ck % 16
                bwk = ck % 4

                @block.sync
                def _(sync):
                    sync.wait_ge(s_sc.h, g_kcast.tok())
                    s_wl01[bwk].inc(sync.dma_start(
                        out=wld[bwk][:, :],
                        in_=ap_wk[dc_k * P:(dc_k + 1) * P, cs_k * 512:(cs_k + 1) * 512]), 16)
                wk_ld.append(s_wl01[bwk].n)

            @block.vector
            def _(v):
                v.wait_ge(xl_sem.h, xl_tok)
                v.wait_ge(s_sc.h, state.get(f"p0xn_{tt-1}", 0))
                s_ve.inc(v.reduce_sum(out=st1[:, 0:1], in_=xz[b][:, :], axis=AX.X))
            s1_tok = s_ve.n

            @block.vector
            def _(v):
                v.tensor_tensor(out=xsq[:, :], in0=xz[b][:, :], in1=xz[b][:, :], op=ALU.mult)
                v.reduce_sum(out=st2[:, 0:1], in_=xsq[:, :], axis=AX.X).then_inc(s_ve.h, 1)
                s_ve.n += 1
                v.tensor_scalar(out=st1[:, 0:1], in0=st1[:, 0:1], scalar1=float(1.0 / D),
                                scalar2=None, op0=ALU.mult).then_inc(s_ve.h, 1)
                s_ve.n += 1
                v.wait_ge(s_ve.h, s_ve.n)
                v.tensor_tensor(out=st3[:, 0:1], in0=st1[:, 0:1], in1=st1[:, 0:1], op=ALU.mult)
                v.tensor_scalar(out=st2[:, 0:1], in0=st2[:, 0:1], scalar1=float(1.0 / D),
                                scalar2=None, op0=ALU.mult).then_inc(s_ve.h, 1)
                s_ve.n += 1
                v.wait_ge(s_ve.h, s_ve.n)
                s_ve.inc(v.tensor_tensor(out=st2[:, 0:1], in0=st2[:, 0:1], in1=st3[:, 0:1],
                                         op=ALU.subtract))
            var_tok = s_ve.n

            if KDBG:
                @block.sync
                def _(sync):
                    sync.wait_ge(s_ve.h, var_tok)
                    s_st.inc(sync.dma_start(out=d_stats[:, 8 * tt:8 * tt + 1], in_=st1[:, 0:1], allow_slow_non_contiguous=True), 16)
                    s_st.inc(sync.dma_start(out=d_stats[:, 8 * tt + 1:8 * tt + 2], in_=st2[:, 0:1], allow_slow_non_contiguous=True), 16)

            @block.scalar
            def _(sc):
                sc.wait_ge(s_ve.h, var_tok)
                s_sc.inc(sc.activation(out=st3[:, 0:1], in_=st2[:, 0:1], func=AF.Sqrt,
                                       bias=np.float32(LN_EPS)))
            std_tok = s_sc.n

            @block.vector
            def _(v):
                v.wait_ge(s_sc.h, std_tok)
                v.reciprocal(out=st4[:, 0:1], in_=st3[:, 0:1]).then_inc(s_ve.h, 1)
                s_ve.n += 1
                v.wait_ge(s_ve.h, s_ve.n)
                s_ve.inc(v.scalar_tensor_tensor(out=st3[:, 0:1], in0=st1[:, 0:1], scalar=-1.0,
                                                in1=st4[:, 0:1], op0=ALU.mult, op1=ALU.mult))
            rstd_tok = s_ve.n

            if KDBG:
                @block.sync
                def _(sync):
                    sync.wait_ge(s_ve.h, rstd_tok)
                    s_st.inc(sync.dma_start(out=d_stats[:, 8 * tt + 2:8 * tt + 3], in_=st4[:, 0:1], allow_slow_non_contiguous=True), 16)
                    s_st.inc(sync.dma_start(out=d_stats[:, 8 * tt + 3:8 * tt + 4], in_=st3[:, 0:1], allow_slow_non_contiguous=True), 16)

            @block.scalar
            def _(sc):
                sc.wait_ge(s_ve.h, rstd_tok)
                s_sc.inc(sc.activation(out=xnz[:, :], in_=xz[b][:, :], func=AF.Identity,
                                       scale=st4[:, 0:1], bias=st3[:, 0:1]))
            state[f"p0xn_{tt}"] = s_sc.n
            g_xz[b].push(s_sc.n)
            xn_tok = s_sc.n

            @block.vector
            def _(v):
                v.wait_ge(s_sc.h, xn_tok)
                v.wait_ge(s_mm.h, g_xbuf.tok())
                v.tensor_reduce(out=st1[:, 0:1], in_=xnz[:, :], axis=AX.X, op=ALU.max,
                                apply_absolute_value=True).then_inc(s_ve.h, 1)
                s_ve.n += 1
                v.wait_ge(s_ve.h, s_ve.n)
                v.tensor_scalar(out=st1[:, 0:1], in0=st1[:, 0:1], scalar1=float(Q_EPS),
                                scalar2=None, op0=ALU.max).then_inc(s_ve.h, 1)
                s_ve.n += 1
                v.wait_ge(s_ve.h, s_ve.n)
                v.reciprocal(out=st2[:, 0:1], in_=st1[:, 0:1]).then_inc(s_ve.h, 1)
                s_ve.n += 1
                v.wait_ge(s_ve.h, s_ve.n)
                v.tensor_scalar(out=sx_all[:, tt:tt + 1], in0=st2[:, 0:1], scalar1=127.0,
                                scalar2=None, op0=ALU.mult)
                v.tensor_scalar(out=sxi_all[:, tt:tt + 1], in0=st1[:, 0:1],
                                scalar1=float(1.0 / 127.0), scalar2=None, op0=ALU.mult)
                s_ve.inc(v.tensor_copy(out=xn_bf[:, :], in_=xnz[:, :]))
            sx_tok = s_ve.n

            if KDBG:
                @block.sync
                def _(sync):
                    sync.wait_ge(s_ve.h, sx_tok)
                    s_st.inc(sync.dma_start(out=d_stats[:, 8 * tt + 4:8 * tt + 5], in_=sx_all[:, tt:tt + 1], allow_slow_non_contiguous=True), 16)
                    s_st.inc(sync.dma_start(out=d_stats[:, 8 * tt + 5:8 * tt + 6], in_=st1[:, 0:1], allow_slow_non_contiguous=True), 16)

            @block.scalar
            def _(sc):
                sc.wait_ge(s_ve.h, sx_tok)
                s_sc.inc(sc.activation(out=xqm[:, :], in_=xnz[:, :], func=AF.Identity,
                                       scale=sx_all[:, tt:tt + 1], bias=np.float32(MAGIC)))
            xqm_tok = s_sc.n

            @block.vector
            def _(v):
                v.wait_ge(s_sc.h, xqm_tok)
                s_ve.inc(v.tensor_scalar(out=xq_bf[:, :], in0=xqm[:, :], scalar1=MAGIC,
                                         scalar2=None, op0=ALU.subtract))
            xqb_tok = s_ve.n

            last_tp_mm = 0
            for which, (src_t, src_tok) in enumerate([(xn_bf, sx_tok), (xq_bf, xqb_tok)]):
                for r in range(2):
                    @block.tensor
                    def _(t):
                        t.wait_ge(s_ve.h, src_tok)
                        t.wait_ge(s_ve.h, g_tp.tok())
                        if tt == 0 and which == 0 and r == 0:
                            t.wait_ge(s_ld.h, const_tok)
                        ins = None
                        for j in range(8):
                            dc = r * 8 + j
                            ins = t.transpose(ps_tp[:, j * P:(j + 1) * P],
                                              src_t[:, dc * P:(dc + 1) * P], ident[:, :])
                        s_mm.inc(ins)
                    tp_tok = s_mm.n
                    last_tp_mm = tp_tok

                    @block.vector
                    def _(v):
                        v.wait_ge(s_mm.h, tp_tok)
                        if which == 0:
                            s_ve.inc(v.tensor_copy(
                                out=xn_fm[:, :].rearrange("p (dc t) -> p dc t", dc=NT)[:, r * 8:(r + 1) * 8, t0:t0 + P],
                                in_=ps_tp[:, :].rearrange("p (j t) -> p j t", j=8)))
                        else:
                            if r == 0:
                                v.wait_ge(s_xqf.h, g_fmst.tok())
                            s_ve.inc(v.tensor_copy(
                                out=fmst[:, r * 1024:(r + 1) * 1024], in_=ps_tp[:, :]))
                    g_tp.push(s_ve.n)
            fm_tok = s_ve.n
            g_xbuf.push(last_tp_mm)

            @block.sync
            def _(sync):
                sync.wait_ge(s_ve.h, fm_tok)
                s_xqf.inc(sync.dma_start(out=d_xqf[t0:t0 + P, :], in_=fmst[:, :]), 16)
            g_fmst.push(s_xqf.n)
            fm_toks.append(fm_tok)

            for j4 in range(4):
                ck = 4 * tt + j4
                cs_k, dc_k = ck // 16, ck % 16
                bwk = ck % 4

                @block.scalar
                def _(sc):
                    sc.wait_ge(s_wl01[bwk].h, wk_ld[ck])
                    s_sc.inc(sc.copy(out=wsl[cs_k][:, dc_k * 512:(dc_k + 1) * 512],
                                     in_=wld[bwk][:, :]))
                g_kcast.push(s_sc.n)
                if dc_k == 15:
                    @block.scalar
                    def _(sc):
                        s_sc.inc(sc.copy(out=stk[:, 0:1], in_=wsl[cs_k][:, 0:1]))
                    cast_tok_k[cs_k] = s_sc.n

            for cs_k in (0, 1):
                if cast_tok_k[cs_k] is None:
                    continue
                while next_k[cs_k] <= tt:
                    rec_k_tile(next_k[cs_k], cs_k)
                    next_k[cs_k] += 1

        join()

        block.flush()
        if PHASE_LIMIT < 3:
            finish()
            return nc
        esA0.close()

        def weight_pass(apw, kind, cs_list=(0, 1, 2, 3), cast_toks=None, seed=None):
            g_ps = [Guard(1), Guard(1)]
            if seed is not None:
                g_ps[0].hist = [seed[0]]
                g_ps[1].hist = [seed[1]]
            g_cast = Guard(2)
            g_evA = Guard(2)
            for cs in cs_list:
                c0 = cs * 512
                bcs = cs
                for dc in (range(16) if cast_toks is None else []):
                    bw2 = dc % 2

                    @block.sync
                    def _(sync):
                        sync.wait_ge(s_sc.h, g_cast.tok())
                        s_wl01[bw2].inc(sync.dma_start(out=wld[bw2][:, :],
                                                in_=apw[dc * P:(dc + 1) * P, c0:c0 + 512]), 16)
                    lt_sem, lt = s_wl01[bw2], s_wl01[bw2].n

                    @block.scalar
                    def _(sc):
                        sc.wait_ge(lt_sem.h, lt)
                        s_sc.inc(sc.copy(out=wsl[bcs][:, dc * 512:(dc + 1) * 512],
                                         in_=wld[bw2][:, :]))
                    g_cast.push(s_sc.n)
                if cast_toks is None:
                    @block.scalar
                    def _(sc):
                        s_sc.inc(sc.copy(out=st1[:, 0:1], in_=wsl[bcs][:, 0:1]))
                    cast_tok = s_sc.n
                else:
                    cast_tok = cast_toks[cs]
                if KDBG and kind == "v":
                    @block.sync
                    def _(sync):
                        sync.wait_ge(s_sc.h, cast_tok)
                        s_st.inc(sync.dma_start(out=d_wsl[cs * P:(cs + 1) * P, :],
                                                in_=wsl[bcs][:, :]), 16)

                for tt in range(NT):
                    pps = [ps_a, ps_b][tt % 2]
                    gi = tt % 2
                    t0 = tt * P

                    @block.tensor
                    def _(t):
                        t.wait_ge(s_sc.h, cast_tok)
                        if kind == "k":
                            t.wait_ge(s_ve.h, g_ps[gi].tok())
                        else:
                            t.wait_ge(s_sc.h, g_ps[gi].tok())
                        for dc in range(16):
                            ins = t.matmul(
                                pps[:, :],
                                xn_fm[:, :].rearrange("p (dc t) -> p dc t", dc=NT)[:, dc, t0:t0 + P],
                                wsl[bcs][:, dc * 512:(dc + 1) * 512],
                                start=(dc == 0), stop=(dc == 15))
                        s_mm.inc(ins)
                    state[f"wsl_{kind}_{cs}"] = s_mm.n
                    mm_tok = s_mm.n

                    if kind == "k":
                        be = tt % 2

                        @block.scalar
                        def _(sc):
                            sc.wait_ge(s_mm.h, mm_tok)
                            sc.wait_ge(s_ve.h, state.get(f"p1dv_{be}", 0))
                            sc.activation(out=ev1[be][:, :], in_=pps[:, :], func=AF.Exp)
                            s_sc.inc(sc.activation(out=ev2[be][:, :], in_=ev1[be][:, :], func=AF.Ln,
                                                   bias=np.float32(1.0)))
                        sp_tok = s_sc.n

                        @block.vector
                        def _(v):
                            v.wait_ge(s_sc.h, sp_tok)
                            v.wait_ge(s_pst01[be].h, state.get(f"p1st_{be}", 0))
                            v.tensor_copy(out=evh[be][:, :], in_=ev2[be][:, :])
                            v.tensor_tensor(out=evf[be][:, :], in0=pps[:, :], in1=ev2[be][:, :],
                                            op=ALU.subtract)
                            s_ve.inc(v.tensor_tensor(out=evl[be][:, :], in0=ev2[be][:, :],
                                                     in1=evh[be][:, :], op=ALU.subtract))
                        state[f"p1dv_{be}"] = s_ve.n
                        g_ps[gi].push(s_ve.n)
                        dv_tok = s_ve.n

                        @block.sync
                        def _(sync):
                            sync.wait_ge(s_ve.h, dv_tok)
                            s_pst01[be].inc(sync.dma_start(out=d_d[t0:t0 + P, c0:c0 + 512], in_=evf[be][:, :]), 16)
                            s_pst01[be].inc(sync.dma_start(out=d_sph[t0:t0 + P, c0:c0 + 512], in_=evh[be][:, :]), 16)
                            s_pst01[be].inc(sync.dma_start(out=d_spl[t0:t0 + P, c0:c0 + 512], in_=evl[be][:, :]), 16)
                        state[f"p1st_{be}"] = s_pst01[be].n
                    else:
                        dst = d_v if kind == "v" else d_q
                        be = tt % 2

                        @block.scalar
                        def _(sc):
                            sc.wait_ge(s_mm.h, mm_tok)
                            sc.wait_ge(s_pso01[be].h, state.get(f"p1st2_{kind}_{be}", 0))
                            fn = AF.Copy if kind == "v" else AF.Relu
                            s_sc.inc(sc.activation(out=evo[be][:, :], in_=pps[:, :], func=fn))
                        g_ps[gi].push(s_sc.n)
                        ev_tok = s_sc.n

                        @block.sync
                        def _(sync):
                            sync.wait_ge(s_sc.h, ev_tok)
                            s_pso01[be].inc(sync.dma_start(out=dst[t0:t0 + P, c0:c0 + 512],
                                                    in_=evo[be][:, :]), 16)
                        state[f"p1st2_{kind}_{be}"] = s_pso01[be].n

        weight_pass(ap_wk, "k", cs_list=(2, 3), cast_toks=cast_tok_k,
                    seed=(state["p1dv_0"], state["p1dv_1"]))
        join()
        block.flush()
        weight_pass(ap_wi, "v")
        join()
        block.flush()
        weight_pass(ap_wq, "q")
        join()

        block.flush()
        esW = es.enter_context(contextlib.ExitStack())
        wql = [sbx(esW, f"s_wql{b}", [P, WQW], F32) for b in range(2)]
        wqt = sbx(esW, "s_wqt", [P, WQW], F32)
        wqb = [sbx(esW, f"s_wqb{b}", [P, WQW], BF16) for b in range(2)]
        wtiles = ([("w1", ap_w1own, r * P, P, c * WQW, WQW) for r in range(2) for c in range(2)] +
                  [("w3", ap_w3own, r * P, P, c * WQW, WQW) for r in range(2) for c in range(2)] +
                  [("w2", ap_w2own, r * P, P, 0, D) for r in range(5)] +
                  [("w2", ap_w2own, 5 * P, 64, 0, D)])

        @block.vector
        def _(v):
            v.memset(wpart[:, :], 0.0)
            v.memset(ones1f[:, :], 1.0)
            s_ve.inc(v.tensor_copy(out=st1[:, 0:1], in_=wpart[:, 0:1]))
        wp0_tok = s_ve.n

        g_wql = [Guard(1), Guard(1)]
        for idx, (wname, apw, r0, rows, c0w, cols) in enumerate(wtiles):
            bw = idx % 2

            @block.sync
            def _(sync):
                sync.wait_ge(s_sc.h, g_wql[bw].tok())
                s_ld.inc(sync.dma_start(out=wql[bw][0:rows, 0:cols],
                                        in_=apw[r0:r0 + rows, c0w:c0w + cols]), 16)
            wl_tok = s_ld.n

            @block.scalar
            def _(sc):
                sc.wait_ge(s_ld.h, wl_tok)
                if idx == 0:
                    sc.wait_ge(s_ve.h, wp0_tok)
                sc.wait_ge(s_ve.h, state.get("wabs_red", 0))
                s_sc.inc(sc.activation(out=wqt[0:rows, 0:cols], in_=wql[bw][0:rows, 0:cols],
                                       func=AF.Abs))
            g_wql[bw].push(s_sc.n)
            wabs_sc = s_sc.n

            @block.vector
            def _(v):
                v.wait_ge(s_sc.h, wabs_sc)
                s_ve.inc(v.reduce_sum(out=wpart[0:rows, idx:idx + 1], in_=wqt[0:rows, 0:cols],
                                      axis=AX.X))
            state["wabs_red"] = s_ve.n
        wabs_tok = s_ve.n
        wabs_sc_final = s_sc.n

        @block.tensor
        def _(t):
            t.wait_ge(s_ve.h, wabs_tok)
            t.wait_ge(s_ld.h, const_tok)
            s_mm.inc(t.matmul(ps_a[0:1, 0:16], ones1f[:, 0:1], wpart[:, :], start=True, stop=True))
        wred_tok = s_mm.n

        @block.vector
        def _(v):
            v.wait_ge(s_mm.h, wred_tok)
            s_ve.inc(v.tensor_copy(out=wsums[0:1, 0:16], in_=ps_a[0:1, 0:16]))
        wsum_tok = s_ve.n

        @block.sync
        def _(sync):
            sync.wait_ge(s_ve.h, wsum_tok)
            s_st.inc(sync.dma_start(out=ar_in[0:1, :], in_=wsums[0:1, :]), 16)
        arin_tok = s_st.n

        @block.gpsimd
        def _(g):
            g.wait_ge(s_st.h, arin_tok)
            s_cc.inc(g.collective_compute("AllReduce", ALU.add, replica_groups=RG,
                                          ins=[ar_in.opt()], outs=[ar_out.opt()]))
        ar_tok = s_cc.n

        @block.sync
        def _(sync):
            sync.wait_ge(s_cc.h, ar_tok)
            s_ld.inc(sync.dma_start(out=warb[:, 0:16], in_=ar_out[0:1, 0:16].broadcast_to([P, 16])), 16)
        arld_tok = s_ld.n

        @block.vector
        def _(v):
            v.wait_ge(s_ld.h, arld_tok)
            v.reduce_sum(out=warb2[:, 0:1], in_=warb[:, 0:4], axis=AX.X).then_inc(s_ve.h, 1)
            v.reduce_sum(out=warb2[:, 1:2], in_=warb[:, 4:8], axis=AX.X).then_inc(s_ve.h, 1)
            v.reduce_sum(out=warb2[:, 2:3], in_=warb[:, 8:14], axis=AX.X).then_inc(s_ve.h, 1)
            s_ve.n += 3
            v.wait_ge(s_ve.h, s_ve.n)
            v.tensor_scalar(out=warb2[:, 4:7], in0=warb2[:, 0:3],
                            scalar1=float(1.0 / (D * DH)), scalar2=float(Q_EPS),
                            op0=ALU.mult, op1=ALU.max).then_inc(s_ve.h, 1)
            s_ve.n += 1
            v.wait_ge(s_ve.h, s_ve.n)
            v.reciprocal(out=wscl[:, 0:3], in_=warb2[:, 4:7]).then_inc(s_ve.h, 1)
            s_ve.n += 1
            v.wait_ge(s_ve.h, s_ve.n)
            v.tensor_scalar(out=wscl[:, 3:4], in0=warb2[:, 6:7],
                            scalar1=float(1.0 / 127.0), scalar2=None, op0=ALU.mult)
            v.tensor_copy(out=wscl[:, 4:6], in_=warb2[:, 4:6]).then_inc(s_ve.h, 1)
            s_ve.n += 1
            v.wait_ge(s_ve.h, s_ve.n)
            s_ve.inc(v.tensor_copy(out=wsums[0:1, 0:8], in_=wscl[0:1, 0:8]))
        wbc_hold = s_ve.n

        if KDBG:
            @block.sync
            def _(sync):
                sync.wait_ge(s_ve.h, wbc_hold)
                s_st.inc(sync.dma_start(out=d_warb[:, 0:16], in_=warb[:, :]), 16)
                s_st.inc(sync.dma_start(out=d_warb[:, 16:24], in_=warb2[:, :]), 16)

        @block.sync
        def _(sync):
            sync.wait_ge(s_ve.h, wbc_hold)
            s_st.inc(sync.dma_start(out=d_scl[0:1, 0:16], in_=wsums[0:1, 0:16]), 16)
        wbc_tok2 = s_st.n
        wbc_tok = s_ld.n
        state["wscl_ready"] = wbc_hold

        g_wqlB = Guard(2)
        g_wqbB = Guard(2)
        for idx, (wname, apw, r0, rows, c0w, cols) in enumerate(wtiles):
            bw = idx % 2
            scol = {"w1": 0, "w3": 1, "w2": 2}[wname]
            dst = {"w1": wag1_in, "w3": wag3_in, "w2": wag2_in}[wname]

            @block.sync
            def _(sync):
                sync.wait_ge(s_sc.h, max(g_wqlB.tok(), wabs_sc_final))
                s_wq01[bw].inc(sync.dma_start(out=wql[bw][0:rows, 0:cols],
                                        in_=apw[r0:r0 + rows, c0w:c0w + cols]), 16)
            wl_sem, wl_tok = s_wq01[bw], s_wq01[bw].n

            @block.scalar
            def _(sc):
                sc.wait_ge(wl_sem.h, wl_tok)
                if idx == 0:
                    sc.wait_ge(s_ve.h, state["wscl_ready"])
                sc.wait_ge(s_ve.h, wq3_toks[-1] if wq3_toks else 0)
                s_sc.inc(sc.activation(out=wqt[0:rows, 0:cols], in_=wql[bw][0:rows, 0:cols],
                                       func=AF.Identity, scale=wscl[0:rows, scol:scol + 1],
                                       bias=np.float32(MAGIC)))
            g_wqlB.push(s_sc.n)
            wq2_tok = s_sc.n

            @block.vector
            def _(v):
                v.wait_ge(s_sc.h, wq2_tok)
                v.wait_ge(s_st.h, g_wqbB.tok())
                v.tensor_scalar(out=wqt[0:rows, 0:cols], in0=wqt[0:rows, 0:cols],
                                scalar1=MAGIC, scalar2=1.0, op0=ALU.subtract, op1=ALU.min)
                s_ve.inc(v.tensor_scalar(out=wqb[bw][0:rows, 0:cols], in0=wqt[0:rows, 0:cols],
                                         scalar1=-1.0, scalar2=None, op0=ALU.max))
            wq3_toks.append(s_ve.n)
            wq3_tok = s_ve.n

            @block.sync
            def _(sync):
                sync.wait_ge(s_ve.h, wq3_tok)
                s_st.inc(sync.dma_start(out=dst[r0:r0 + rows, c0w:c0w + cols],
                                        in_=wqb[bw][0:rows, 0:cols]), 16)
            g_wqbB.push(s_st.n)
        wqdone_tok = s_st.n

        @block.gpsimd
        def _(g):
            g.wait_ge(s_st.h, wqdone_tok)
            s_cc.inc(g.collective_compute("AllGather", ALU.bypass, replica_groups=RG,
                                          ins=[wag1_in.opt()], outs=[wag1_out.opt()]))
            s_cc.inc(g.collective_compute("AllGather", ALU.bypass, replica_groups=RG,
                                          ins=[wag3_in.opt()], outs=[wag3_out.opt()]))
            s_cc.inc(g.collective_compute("AllGather", ALU.bypass, replica_groups=RG,
                                          ins=[wag2_in.opt()], outs=[wag2_out.opt()]))
        wag_tok = s_cc.n
        @block.vector
        def _(v):
            v.wait_ge(s_ve.h, state["wscl_ready"])
            v.tensor_tensor(out=rs1_all[:, :], in0=sxi_all[:, :],
                            in1=wscl[:, 4:5].broadcast_to([P, NT]), op=ALU.mult)
            s_ve.inc(v.tensor_tensor(out=rs3_all[:, :], in0=sxi_all[:, :],
                                     in1=wscl[:, 5:6].broadcast_to([P, NT]), op=ALU.mult))
        rsff_tok = s_ve.n

        esW.close()

        block.flush()
        if PHASE_LIMIT < 4:
            finish()
            return nc
        esA.close()
        esB = es.enter_context(contextlib.ExitStack())
        sphi = [sbx(esB, f"s_sphi{b}", [P, D], BF16) for b in range(2)]
        splo = [sbx(esB, f"s_splo{b}", [P, D], BF16) for b in range(2)]
        dt_ = [sbx(esB, f"s_dt{b}", [P, D], F32) for b in range(2)]
        qin = [sbx(esB, f"s_qin{b}", [P, D], BF16) for b in range(2)]
        vt_ = [sbx(esB, f"s_vt{b}", [P, D], BF16) for b in range(2)]
        ka_tm = sbx(esB, "s_ka_tm", [P, D], BF16)
        ks_tm = sbx(esB, "s_ks_tm", [P, D], BF16)
        qt_tm = sbx(esB, "s_qt_tm", [P, D], BF16)
        eq_sc = sbx(esB, "s_eq_sc", [P, 512], F32)
        tsc = sbx(esB, "s_tsc", [P, 512], F32)
        tsc2 = sbx(esB, "s_tsc2", [P, 1024], F32)
        ka_fm = sbx(esB, "s_ka_fm", [P, D], BF16)
        qt_fm = sbx(esB, "s_qt_fm", [P, D], BF16)
        qh_fm = sbx(esB, "s_qh_fm", [P, D], BF16)
        pm_sb = sbx(esB, "s_pm_sb", [P, 1024], BF16)
        a_sb = sbx(esB, "s_a_sb", [P, 32], F32)
        ea_sb = sbx(esB, "s_ea_sb", [P, 32], F32)
        er_sb = sbx(esB, "s_er_sb", [P, NH], F32)
        rn_sb = sbx(esB, "s_rn_sb", [P, NH], F32)
        s_sb = sbx(esB, "s_s_sb", [P, D], F32)
        sbf_sb = sbx(esB, "s_sbf_sb", [P, D], BF16)
        o_st = sbx(esB, "s_o_st", [P, 1024], BF16)
        pubst = sbx(esB, "s_pubst", [P, 2064], F32)

        @block.vector
        def _(v):
            v.memset(s_sb[:, :], 0.0)
            v.memset(rn_sb[:, :], 0.0)
            s_ve.inc(v.tensor_copy(out=sbf_sb[:], in_=s_sb[:]))

        p2 = dict(last_ks=0, last_qt=0, prev_er=0, prev_tend=0,
                  prev_ost=0, prev_qh=0, cs_cons=[0, 0], cs_cons_sc=[0, 0],
                  sa_cons=0, p_cons=0, o_cons=0, tp_cons=0)

        for i in range(NT):
            b = i % 2
            t0 = i * P

            @block.sync
            def _(sync):
                if i == 0:
                    sync.wait_ge(s_st.h, wqdone_tok)
                if i >= 2:
                    sync.wait_ge(s_fr.h, i - 1)
                s_p2l01[b].inc(sync.dma_start(out=sphi[b][:], in_=d_sph[t0:t0 + P, :]), 16)
                s_p2l01[b].inc(sync.dma_start(out=splo[b][:], in_=d_spl[t0:t0 + P, :]), 16)
                s_p2l01[b].inc(sync.dma_start(out=dt_[b][:], in_=d_d[t0:t0 + P, :]), 16)
                s_p2l01[b].inc(sync.dma_start(out=qin[b][:], in_=d_q[t0:t0 + P, :]), 16)
                s_p2l01[b].inc(sync.dma_start(out=vt_[b][:], in_=d_v[t0:t0 + P, :]), 16)
            ld_sem, ld_tok = s_p2l01[b], s_p2l01[b].n
            prev_fence = s_mm.n

            for s in range(4):
                sl = slice(512 * s, 512 * (s + 1))
                pcs = [ps_a, ps_b][s % 2]

                @block.tensor
                def _(t):
                    t.wait_ge(ld_sem.h, ld_tok)
                    t.wait_ge(s_ve.h, p2["cs_cons"][s % 2])
                    t.wait_ge(s_sc.h, p2["cs_cons_sc"][s % 2])
                    t.matmul(pcs[:, :], tri_i[:, :], sphi[b][:, sl], start=True, stop=False)
                    s_mm.inc(t.matmul(pcs[:, :], tri_i[:, :], splo[b][:, sl], start=False, stop=True))
                csp_tok = s_mm.n

                @block.vector
                def _(v):
                    v.wait_ge(s_mm.h, csp_tok)
                    v.wait_ge(s_sc.h, p2["last_ks"])
                    s_ve.inc(v.tensor_tensor(out=tsc[:, :], in0=pcs[:, :], in1=dt_[b][:, sl], op=ALU.add))
                tka_tok = s_ve.n

                @block.scalar
                def _(sc):
                    sc.wait_ge(s_ve.h, tka_tok)
                    sc.wait_ge(s_ve.h, p2["last_qt"])
                    sc.wait_ge(s_mm.h, prev_fence)
                    sc.activation(out=ka_tm[:, sl], in_=tsc[:, :], func=AF.Exp)
                    s_sc.inc(sc.activation(out=eq_sc[:, :], in_=pcs[:, :], func=AF.Exp,
                                           scale=-1.0, bias=np.float32(-C0)))
                eq_tok = s_sc.n

                @block.vector
                def _(v):
                    v.wait_ge(s_sc.h, eq_tok)
                    s_ve.inc(v.tensor_tensor(out=qt_tm[:, sl], in0=qin[b][:, sl], in1=eq_sc[:, :], op=ALU.mult))
                p2["last_qt"] = s_ve.n

                @block.tensor
                def _(t):
                    t.wait_ge(s_ve.h, tka_tok)
                    t.wait_ge(s_sc.h, eq_tok)
                    t.matmul(pcs[:, :], tri_x[:, :], sphi[b][:, sl], start=True, stop=False)
                    s_mm.inc(t.matmul(pcs[:, :], tri_x[:, :], splo[b][:, sl], start=False, stop=True))
                csx_tok = s_mm.n

                @block.vector
                def _(v):
                    v.wait_ge(s_mm.h, csx_tok)
                    s_ve.inc(v.tensor_tensor(out=tsc[:, :], in0=dt_[b][:, sl], in1=pcs[:, :], op=ALU.subtract))
                tks_tok = s_ve.n
                p2["cs_cons"][s % 2] = tks_tok

                @block.scalar
                def _(sc):
                    sc.wait_ge(s_ve.h, tks_tok)
                    s_sc.inc(sc.activation(out=ks_tm[:, sl], in_=tsc[:, :], func=AF.Exp))
                p2["cs_cons_sc"][s % 2] = s_sc.n
                p2["last_ks"] = s_sc.n
            last_qtl = p2["last_qt"]

            @block.tensor
            def _(t):
                t.wait_ge(s_ve.h, p2["sa_cons"])
                t.wait_ge(ld_sem.h, ld_tok)
                ins = None
                for h in range(NH):
                    t.matmul(ps_c[:, h * 2:h * 2 + 2], sphi[b][:, h * P:(h + 1) * P],
                             csel[:, 0:2], start=True, stop=False)
                    ins = t.matmul(ps_c[:, h * 2:h * 2 + 2], splo[b][:, h * P:(h + 1) * P],
                                   csel[:, 0:2], start=False, stop=True)
                s_mm.inc(ins)
            sa_tok = s_mm.n

            @block.vector
            def _(v):
                v.wait_ge(s_mm.h, sa_tok)
                v.wait_ge(s_sc.h, p2["prev_er"])
                s_ve.inc(v.tensor_copy(out=a_sb[:, :].rearrange("p (c h) -> p c h", c=2),
                                       in_=ps_c[:, 0:32].rearrange("p (h c) -> p c h", h=NH)))
            a_tok = s_ve.n
            p2["sa_cons"] = a_tok

            @block.scalar
            def _(sc):
                sc.wait_ge(s_ve.h, a_tok)
                sc.wait_ge(s_ve.h, p2["prev_tend"])
                sc.activation(out=ea_sb[:, :], in_=a_sb[:, :], func=AF.Exp, scale=-1.0)
                s_sc.inc(sc.activation(out=er_sb[:, :], in_=rn_sb[:, :], func=AF.Exp, scale=-1.0))
            er_tok = s_sc.n
            p2["prev_er"] = er_tok

            for src_t, dst_t in [(qt_tm, qt_fm), (ka_tm, ka_fm)]:
                for r in range(2):
                    @block.tensor
                    def _(t):
                        if src_t is qt_tm:
                            t.wait_ge(s_ve.h, last_qtl)
                        else:
                            t.wait_ge(s_sc.h, p2["cs_cons_sc"][1])
                        t.wait_ge(s_ve.h, p2["tp_cons"])
                        ins = None
                        for j in range(8):
                            h = r * 8 + j
                            ins = t.transpose(ps_tp[:, j * P:(j + 1) * P],
                                              src_t[:, h * P:(h + 1) * P], ident[:, :])
                        s_mm.inc(ins)
                    tp_tok = s_mm.n

                    @block.vector
                    def _(v):
                        v.wait_ge(s_mm.h, tp_tok)
                        s_ve.inc(v.tensor_copy(out=dst_t[:, r * 1024:(r + 1) * 1024], in_=ps_tp[:, :]))
                    p2["tp_cons"] = s_ve.n
            fm_tok = s_ve.n

            @block.vector
            def _(v):
                v.wait_ge(s_sc.h, er_tok)
                v.wait_ge(s_qhst.h, p2["prev_qh"])
                v.tensor_tensor(
                    out=qh_fm[:, :].rearrange("p (h t) -> p h t", h=NH),
                    in0=qt_fm[:, :].rearrange("p (h t) -> p h t", h=NH),
                    in1=er_sb[:, :].rearrange("p (h o) -> p h o", o=1).broadcast_to([P, NH, P]),
                    op=ALU.mult)
                s_ve.inc(v.tensor_tensor(
                    out=qh_fm[:, :].rearrange("p (h c t) -> p h c t", h=NH, c=2)[:, :, 1:2, :],
                    in0=qh_fm[:, :].rearrange("p (h c t) -> p h c t", h=NH, c=2)[:, :, 1:2, :],
                    in1=ea_sb[:, 0:NH].rearrange("p (h o) -> p h o", o=1)
                        .rearrange("p h (a o) -> p h a o", a=1).broadcast_to([P, NH, 1, C]),
                    op=ALU.mult))
            qh_tok = s_ve.n

            @block.sync
            def _(sync):
                sync.wait_ge(s_ve.h, qh_tok)
                s_qhst.inc(sync.dma_start(out=d_qh[:, i * D:(i + 1) * D], in_=qh_fm[:, :]), 16)
            p2["prev_qh"] = s_qhst.n

            for c in range(2):
                cb = c * C
                ch = 2 * i + c

                @block.tensor
                def _(t):
                    t.wait_ge(s_ve.h, fm_tok)
                    t.wait_ge(s_ve.h, p2["p_cons"])
                    ins = None
                    for h in range(NH):
                        ins = t.matmul(ps_e2[cb:cb + C, h * C:(h + 1) * C],
                                       ka_fm[:, h * P + cb:h * P + cb + C],
                                       qt_fm[:, h * P + cb:h * P + cb + C],
                                       start=True, stop=True)
                    s_mm.inc(ins)
                pt_tok = s_mm.n

                @block.vector
                def _(v):
                    v.wait_ge(s_mm.h, pt_tok)
                    s_ve.inc(v.scalar_tensor_tensor(out=pm_sb[cb:cb + C, :], in0=ps_e2[cb:cb + C, :],
                                                    scalar=1.0, in1=maskp[cb:cb + C, :],
                                                    op0=ALU.bypass, op1=ALU.mult))
                pm_tok = s_ve.n
                p2["p_cons"] = pm_tok

                @block.tensor
                def _(t):
                    t.wait_ge(s_ve.h, pm_tok)
                    t.wait_ge(s_ve.h, p2["o_cons"])
                    ins = None
                    for h in range(NH):
                        t.matmul(ps_d2[:, h * C:(h + 1) * C],
                                 vt_[b][cb:cb + C, h * P:(h + 1) * P],
                                 pm_sb[cb:cb + C, h * C:(h + 1) * C],
                                 start=True, stop=False)
                        ins = t.matmul(ps_d2[:, h * C:(h + 1) * C],
                                       sbf_sb[:, h * P:(h + 1) * P],
                                       qt_fm[:, h * P + cb:h * P + cb + C],
                                       start=False, stop=True)
                    s_mm.inc(ins)
                o_tok = s_mm.n

                @block.vector
                def _(v):
                    v.wait_ge(s_mm.h, o_tok)
                    v.wait_ge(s_ost.h, p2["prev_ost"])
                    s_ve.inc(v.tensor_copy(out=o_st[:, :], in_=ps_d2[:, :]))
                oev_tok = s_ve.n
                p2["o_cons"] = oev_tok

                @block.sync
                def _(sync):
                    sync.wait_ge(s_ve.h, oev_tok)
                    s_ost.inc(sync.dma_start(out=d_ol[ch * P:(ch + 1) * P, :], in_=o_st[:, :]), 16)
                p2["prev_ost"] = s_ost.n

                for hh in range(2):
                    hb = hh * 8

                    @block.tensor
                    def _(t):
                        t.wait_ge(s_ve.h, p2["o_cons"])
                        t.wait_ge(s_sc.h, p2["cs_cons_sc"][1])
                        ins = None
                        for j in range(8):
                            h = hb + j
                            ins = t.matmul(ps_d2[:, j * P:(j + 1) * P],
                                           ks_tm[cb:cb + C, h * P:(h + 1) * P],
                                           vt_[b][cb:cb + C, h * P:(h + 1) * P],
                                           start=True, stop=True)
                        s_mm.inc(ins)
                    u_tok = s_mm.n

                    @block.vector
                    def _(v):
                        v.wait_ge(s_mm.h, u_tok)
                        hsl = slice(hb * P, (hb + 8) * P)
                        v.tensor_tensor(
                            out=tsc2[:, :].rearrange("p (h t) -> p h t", h=8),
                            in0=s_sb[:, hsl].rearrange("p (h t) -> p h t", h=8),
                            in1=ea_sb[:, c * NH + hb:c * NH + hb + 8]
                                .rearrange("p (h o) -> p h o", o=1).broadcast_to([P, 8, P]),
                            op=ALU.mult)
                        s_ve.inc(v.tensor_tensor(out=s_sb[:, hsl], in0=tsc2[:, :], in1=ps_d2[:, :], op=ALU.add))
                    p2["o_cons"] = s_ve.n

                @block.vector
                def _(v):
                    v.wait_ge(s_sc.h, er_tok)
                    v.tensor_copy(out=sbf_sb[:], in_=s_sb[:])
                    s_ve.inc(v.tensor_tensor(out=rn_sb[:, :], in0=rn_sb[:, :],
                                             in1=a_sb[:, c * NH:(c + 1) * NH], op=ALU.add))
                p2["o_cons"] = s_ve.n

            @block.vector
            def _(v):
                s_fr.inc(v.tensor_copy(out=tsc[:, 0:1], in_=tsc[:, 0:1]))
            p2["prev_tend"] = s_ve.n
            if i % 4 == 3:
                block.flush()

        @block.vector
        def _(v):
            v.wait_ge(s_fr.h, NT)
            v.tensor_copy(out=pubst[:, 0:D], in_=s_sb[:, :])
            s_ve.inc(v.tensor_copy(out=pubst[:, D:D + NH], in_=rn_sb[:, :]))
        pub_tok = s_ve.n

        @block.sync
        def _(sync):
            sync.wait_ge(s_ve.h, pub_tok)
            s_st.inc(sync.dma_start(out=pub_in[:, :], in_=pubst[:, :]), 16)
        pubst_tok = s_st.n

        @block.gpsimd
        def _(g):
            g.wait_ge(s_st.h, pubst_tok)
            s_cc.inc(g.collective_compute("AllGather", ALU.bypass, replica_groups=RG,
                                          ins=[pub_in.opt()], outs=[pub_out.opt()]))
        pubag_tok = s_cc.n

        join()

        block.flush()
        if PHASE_LIMIT < 5:
            finish()
            return nc
        esB.close()
        esC = es.enter_context(contextlib.ExitStack())
        xqt = [sbx(esC, f"s_xqt{t}", [P, D], BF16) for t in range(4)]
        hsb = [sbx(esC, f"s_hsb{t}", [P, DH], F32) for t in range(4)]
        w1s = [sbx(esC, f"s_w1s{b}", [P, 16 * 512], BF16) for b in range(2)]
        w3s = [sbx(esC, f"s_w3s{b}", [P, 16 * 512], BF16) for b in range(2)]
        h1sl = sbx(esC, "s_h1sl", [P, 512], F32)
        hqs = [sbx(esC, f"s_hqs{i}", [P, 512], F32) for i in range(2)]
        hq_bf = sbx(esC, "s_hq_bf", [P, DH], BF16)
        hqf_st = sbx(esC, "s_hqf_st", [P, DH], BF16)
        hmx = sbx(esC, "s_hmx", [P, 1], F32)
        amax4 = sbx(esC, "s_amax4", [P, 44], F32)
        chv = sbx(esC, "s_chv", [P, 1], F32)

        @block.tensor
        def _(t):
            t.wait_ge(s_cc.h, wag_tok)

        g_w13 = [Guard(1), Guard(1)]
        g_y1 = [Guard(1), Guard(1)]
        g_y3 = [Guard(1), Guard(1)]
        g_h1sl = Guard(1)
        g_hqm = [Guard(1), Guard(1)]
        g_chv = Guard(1)
        g_hqbf = Guard(1)
        g_tp4 = Guard(1)
        g_hqfst = Guard(1)
        g_xqt = Guard(1)
        for st_i in range(P4_NST):
            tt0 = st_i * 4

            @block.sync
            def _(sync):
                sync.wait_ge(s_mm.h, g_xqt.tok())
                for tl in range(4):
                    r0 = (tt0 + tl) * P
                    s_xq4.inc(sync.dma_start(out=xqt[tl][:, :], in_=d_xqf[r0:r0 + P, :]), 16)
            xq_tok = s_xq4.n

            hmm_last = 0
            for hs in range(11):
                bw = hs % 2
                h0 = hs * 512

                @block.sync
                def _(sync):
                    sync.wait_ge(s_mm.h, g_w13[bw].tok())
                    for dc in range(16):
                        s_w13[bw].inc(sync.dma_start(
                            out=w1s[bw][:, dc * 512:(dc + 1) * 512],
                            in_=wag1_out[dc * P:(dc + 1) * P, h0:h0 + 512]), 16)
                        s_w13[bw].inc(sync.dma_start(
                            out=w3s[bw][:, dc * 512:(dc + 1) * 512],
                            in_=wag3_out[dc * P:(dc + 1) * P, h0:h0 + 512]), 16)
                w_sem, w_tok = s_w13[bw], s_w13[bw].n

                for tl in range(4):
                    gy = tl % 2
                    ppy1 = [ps_a, ps_b][gy]
                    ppy3v = [ps_c[:, :], ps_d2[:, 0:512]][gy]
                    ttg = tt0 + tl

                    @block.tensor
                    def _(t):
                        t.wait_ge(w_sem.h, w_tok)
                        t.wait_ge(s_xq4.h, xq_tok)
                        t.wait_ge(s_sc.h, g_y1[gy].tok())
                        t.wait_ge(s_ve.h, g_y3[gy].tok())
                        for dc in range(16):
                            t.matmul(ppy1[:, :], xqt[tl][:, dc * P:(dc + 1) * P],
                                     w1s[bw][:, dc * 512:(dc + 1) * 512],
                                     start=(dc == 0), stop=(dc == 15))
                        ins = None
                        for dc in range(16):
                            ins = t.matmul(ppy3v, xqt[tl][:, dc * P:(dc + 1) * P],
                                           w3s[bw][:, dc * 512:(dc + 1) * 512],
                                           start=(dc == 0), stop=(dc == 15))
                        s_mm.inc(ins)
                    y_tok = s_mm.n
                    hmm_last = y_tok

                    if not P4_NOEV:
                        @block.scalar
                        def _(sc):
                            sc.wait_ge(s_mm.h, y_tok)
                            if st_i == 0 and hs == 0 and tl == 0:
                                sc.wait_ge(s_ve.h, rsff_tok)
                            sc.wait_ge(s_ve.h, g_h1sl.tok())
                            scl = rs1_all[:, ttg:ttg + 1] if P4_EVMODE >= 3 else 1.0
                            s_sc.inc(sc.activation(out=h1sl[:, :], in_=ppy1[:, :], func=AF.Silu,
                                                   scale=scl))
                        g_y1[gy].push(s_sc.n)
                        h1_tok = s_sc.n

                        if P4_EVMODE >= 2:
                            @block.vector
                            def _(v):
                                v.wait_ge(s_sc.h, h1_tok)
                                scl2 = rs3_all[:, ttg:ttg + 1] if P4_EVMODE >= 3 else 1.0
                                s_ve.inc(v.scalar_tensor_tensor(
                                    out=hsb[tl][:, h0:h0 + 512], in0=ppy3v,
                                    scalar=scl2, in1=h1sl[:, :],
                                    op0=ALU.mult, op1=ALU.mult))
                                s_ve.inc(v.tensor_reduce(
                                    out=amax4[:, tl * 11 + hs:tl * 11 + hs + 1],
                                    in_=hsb[tl][:, h0:h0 + 512], axis=AX.X, op=ALU.max,
                                    apply_absolute_value=True))
                            g_y3[gy].push(s_ve.n)
                            g_h1sl.push(s_ve.n)
                g_w13[bw].push(hmm_last)
            hdone_tok = s_ve.n
            g_xqt.push(hmm_last)

            for tl in range(4 if not P4_NOHQ else 0):
                ttg = tt0 + tl

                @block.vector
                def _(v):
                    v.wait_ge(s_ve.h, hdone_tok)
                    v.wait_ge(s_sc.h, g_chv.tok())
                    v.tensor_reduce(out=hmx[:, 0:1], in_=amax4[:, tl * 11:(tl + 1) * 11],
                                    axis=AX.X, op=ALU.max,
                                    apply_absolute_value=True).then_inc(s_ve.h, 1)
                    s_ve.n += 1
                    v.wait_ge(s_ve.h, s_ve.n)
                    v.tensor_scalar(out=hmx[:, 0:1], in0=hmx[:, 0:1], scalar1=float(Q_EPS),
                                    scalar2=None, op0=ALU.max).then_inc(s_ve.h, 1)
                    s_ve.n += 1
                    v.wait_ge(s_ve.h, s_ve.n)
                    v.reciprocal(out=chv[:, 0:1], in_=hmx[:, 0:1]).then_inc(s_ve.h, 1)
                    s_ve.n += 1
                    v.wait_ge(s_ve.h, s_ve.n)
                    v.tensor_scalar(out=chv[:, 0:1], in0=chv[:, 0:1], scalar1=127.0,
                                    scalar2=None, op0=ALU.mult).then_inc(s_ve.h, 1)
                    s_ve.n += 1
                    v.wait_ge(s_ve.h, s_ve.n)
                    s_ve.inc(v.tensor_tensor(out=rs2_all[:, ttg:ttg + 1], in0=hmx[:, 0:1],
                                             in1=wscl[:, 3:4], op=ALU.mult))
                hq1_tok = s_ve.n

                for hs2 in range(11):
                    bq = hs2 % 2
                    hsl2 = slice(hs2 * 512, (hs2 + 1) * 512)

                    @block.scalar
                    def _(sc):
                        if hs2 == 0:
                            sc.wait_ge(s_ve.h, hq1_tok)
                            sc.wait_ge(s_mm.h, g_tp4.hist[0])
                        sc.wait_ge(s_ve.h, g_hqm[bq].tok())
                        s_sc.inc(sc.activation(out=hqs[bq][:, :], in_=hsb[tl][:, hsl2], func=AF.Identity,
                                               scale=chv[:, 0:1], bias=np.float32(MAGIC)))
                    hqm_tok = s_sc.n

                    @block.vector
                    def _(v):
                        v.wait_ge(s_sc.h, hqm_tok)
                        s_ve.inc(v.tensor_scalar(out=hq_bf[:, hsl2], in0=hqs[bq][:, :], scalar1=MAGIC,
                                                 scalar2=None, op0=ALU.subtract))
                    g_hqm[bq].push(s_ve.n)
                g_chv.push(s_sc.n)
                hqb_tok = s_ve.n

                last_tp = 0
                for r in range(6):
                    n_in_r = 8 if r < 5 else 4

                    @block.tensor
                    def _(t):
                        t.wait_ge(s_ve.h, hqb_tok)
                        t.wait_ge(s_ve.h, state.get("p4tp_cons", 0))
                        ins = None
                        for j in range(n_in_r):
                            hc = r * 8 + j
                            ins = t.transpose(ps_tp[:, j * P:(j + 1) * P],
                                              hq_bf[:, hc * P:(hc + 1) * P], ident[:, :])
                        s_mm.inc(ins)
                    tp_tok = s_mm.n
                    last_tp = tp_tok

                    @block.vector
                    def _(v):
                        v.wait_ge(s_mm.h, tp_tok)
                        if r == 0:
                            v.wait_ge(s_hqst.h, g_hqfst.tok())
                        s_ve.inc(v.tensor_copy(out=hqf_st[:, r * 1024:r * 1024 + n_in_r * P],
                                               in_=ps_tp[:, 0:n_in_r * P]))
                    state["p4tp_cons"] = s_ve.n
                g_tp4.push(last_tp)
                hqf_tok = s_ve.n

                @block.sync
                def _(sync):
                    sync.wait_ge(s_ve.h, hqf_tok)
                    r0 = ttg * P
                    s_hqst.inc(sync.dma_start(out=d_hq[r0:r0 + P, :], in_=hqf_st[:, :]), 16)
                g_hqfst.push(s_hqst.n)

        join()

        block.flush()
        if PHASE_LIMIT < 6:
            finish()
            return nc
        esC.close()
        esD = es.enter_context(contextlib.ExitStack())
        w2s = [sbx(esD, f"s_w2s{d}", [P, 44 * 512], BF16) for d in range(4)]
        hqin = [sbx(esD, f"s_hqin{b}", [P, DH], BF16) for b in range(2)]
        gt_st = [sbx(esD, f"s_gt_st{b}", [P, 512], BF16) for b in range(2)]

        w2_toks = []

        @block.sync
        def _(sync):
            for do in range(4):
                o0 = do * 512
                s_w2l.inc(sync.dma_start(
                    out=w2s[do][:, :].rearrange("p (hc n) -> p hc n", hc=44),
                    in_=wag2_out[:, o0:o0 + 512].rearrange("(hc p) n -> p hc n", p=P)), 16)
                w2_toks.append(s_w2l.n)

        g_hqin = [Guard(1), Guard(1)]
        g_py = [Guard(1), Guard(1)]
        g_gt = [Guard(1), Guard(1)]
        for tt in range(NT):
            bh = tt % 2
            r0 = tt * P

            @block.sync
            def _(sync):
                sync.wait_ge(s_mm.h, g_hqin[bh].tok())
                s_hql01[bh].inc(sync.dma_start(out=hqin[bh][:, :], in_=d_hq[r0:r0 + P, :]), 16)
            hq_tok = s_hql01[bh].n

            last_mm = 0
            for do in range(4):
                o0 = do * 512
                bp = do % 2
                ppy = [ps_a, ps_b][bp]

                @block.tensor
                def _(t):
                    t.wait_ge(s_w2l.h, w2_toks[do])
                    if do == 0:
                        t.wait_ge(s_hql01[bh].h, hq_tok)
                    t.wait_ge(s_sc.h, g_py[bp].tok())
                    ins = None
                    for hc in range(44):
                        ins = t.matmul(ppy[:, :], hqin[bh][:, hc * P:(hc + 1) * P],
                                       w2s[do][:, hc * 512:(hc + 1) * 512],
                                       start=(hc == 0), stop=(hc == 43))
                    s_mm.inc(ins)
                y2_tok = s_mm.n
                last_mm = y2_tok

                @block.scalar
                def _(sc):
                    sc.wait_ge(s_mm.h, y2_tok)
                    sc.wait_ge(s_gt01[bp].h, g_gt[bp].tok())
                    s_sc.inc(sc.activation(out=gt_st[bp][:, :], in_=ppy[:, :], func=AF.Silu,
                                           scale=rs2_all[:, tt:tt + 1]))
                g_py[bp].push(s_sc.n)
                gt_tok = s_sc.n

                @block.sync
                def _(sync):
                    sync.wait_ge(s_sc.h, gt_tok)
                    s_gt01[bp].inc(sync.dma_start(out=d_gate[r0:r0 + P, o0:o0 + 512], in_=gt_st[bp][:, :]), 16)
                g_gt[bp].push(s_gt01[bp].n)
            g_hqin[bh].push(last_mm)

        join()

        block.flush()
        if PHASE_LIMIT < 7:
            finish()
            return nc
        esD.close()
        esE = es.enter_context(contextlib.ExitStack())
        wo_sb = sbx(esE, "s_wo_sb", [P, NT * D], BF16)
        wost = [sbx(esE, f"s_wost{b}", [P, D], F32) for b in range(2)]
        sinit = sbx(esE, "s_sinit", [P, D], F32)
        sinit_bf = sbx(esE, "s_sinit_bf", [P, D], BF16)
        uj = sbx(esE, "s_uj", [P, 2064], F32)
        bneg = sbx(esE, "s_bneg", [P, NH], F32)
        ejv = sbx(esE, "s_ejv", [P, NH], F32)
        tsc2 = sbx(esE, "s_tsc2e", [P, 1024], F32)
        ol_in = [sbx(esE, f"s_ol_in{b}", [P, 2048], BF16) for b in range(2)]
        qh_in = [sbx(esE, f"s_qh_in{b}", [P, D], BF16) for b in range(2)]
        gt_in = [sbx(esE, f"s_gt_in{b}", [P, D], BF16) for b in range(2)]
        x6 = [sbx(esE, f"s_x6{b}", [P, D], F32) for b in range(2)]
        of_sb = sbx(esE, "s_of_sb", [P, D], F32)
        o2_sb = sbx(esE, "s_o2_sb", [P, D], BF16)
        ob_sb = sbx(esE, "s_ob_sb", [P, D], BF16)
        rr_sb = sbx(esE, "s_rr_sb", [P, 1], F32)
        ss_sb = sbx(esE, "s_ss_sb", [P, 1], F32)
        sv_sb = sbx(esE, "s_sv_sb", [P, 512], F32)
        sg_sb = sbx(esE, "s_sg_sb", [P, 512], F32)
        out_sb = sbx(esE, "s_out_sb", [P, D], F32)
        g_wost = [Guard(1), Guard(1)]
        for dc in range(16):
            bw2 = dc % 2

            @block.sync
            def _(sync):
                sync.wait_ge(s_ve.h, g_wost[bw2].tok())
                s_wol01[bw2].inc(sync.dma_start(out=wost[bw2][:, :], in_=ap_wo[dc * P:(dc + 1) * P, :]), 16)
            wol_tok = s_wol01[bw2].n

            @block.vector
            def _(v):
                v.wait_ge(s_wol01[bw2].h, wol_tok)
                s_ve.inc(v.tensor_copy(out=wo_sb[:, dc * D:(dc + 1) * D], in_=wost[bw2][:, :]))
            g_wost[bw2].push(s_ve.n)
        wo_tok = s_ve.n

        @block.vector
        def _(v):
            v.memset(sinit[:, :], 0.0)
            s_ve.inc(v.memset(bneg[:, :], 0.0))
        si_prev = s_ve.n

        for j in range(NCORE - 1, -1, -1):
            @block.sync
            def _(sync):
                sync.wait_ge(s_cc.h, pubag_tok)
                sync.wait_ge(s_ve.h, si_prev)
                s_uj.inc(sync.dma_start(out=uj[:, :], in_=pub_out[j * P:(j + 1) * P, :]), 16)
            uj_tok = s_uj.n

            @block.scalar
            def _(sc):
                sc.wait_ge(s_ve.h, si_prev)
                s_sc.inc(sc.activation(out=ejv[:, :], in_=bneg[:, :], func=AF.Exp, scale=-1.0))
            ej_tok = s_sc.n

            @block.vector
            def _(v):
                v.wait_ge(s_sc.h, ej_tok)
                v.wait_ge(s_uj.h, uj_tok)
                v.wait_ge(s_ld.h, const_tok)
                v.tensor_tensor(out=ejv[:, :], in0=ejv[:, :],
                                in1=cmsk[:, j:j + 1].broadcast_to([P, NH]), op=ALU.mult)
                v.tensor_tensor(out=tsc2[:, 0:1024].rearrange("p (h t) -> p h t", h=8),
                                in0=uj[:, 0:1024].rearrange("p (h t) -> p h t", h=8),
                                in1=ejv[:, 0:8].rearrange("p (h o) -> p h o", o=1).broadcast_to([P, 8, P]),
                                op=ALU.mult)
                v.tensor_tensor(out=sinit[:, 0:1024], in0=sinit[:, 0:1024], in1=tsc2[:, 0:1024], op=ALU.add)
                v.tensor_tensor(out=tsc2[:, 0:1024].rearrange("p (h t) -> p h t", h=8),
                                in0=uj[:, 1024:2048].rearrange("p (h t) -> p h t", h=8),
                                in1=ejv[:, 8:16].rearrange("p (h o) -> p h o", o=1).broadcast_to([P, 8, P]),
                                op=ALU.mult)
                v.tensor_tensor(out=sinit[:, 1024:2048], in0=sinit[:, 1024:2048], in1=tsc2[:, 0:1024], op=ALU.add)
                v.tensor_tensor(out=ejv[:, :], in0=uj[:, D:D + NH],
                                in1=cmsk[:, j:j + 1].broadcast_to([P, NH]), op=ALU.mult)
                s_ve.inc(v.tensor_tensor(out=bneg[:, :], in0=bneg[:, :], in1=ejv[:, :], op=ALU.add))
            si_prev = s_ve.n

        @block.vector
        def _(v):
            s_ve.inc(v.tensor_copy(out=sinit_bf[:, :], in_=sinit[:, :]))
        sib_tok = s_ve.n

        g_p6in = [Guard(1), Guard(1)]
        g_corr = Guard(1)
        g_of = Guard(1)
        g_ob = Guard(1)
        g_ssq = Guard(1)
        g_ps6 = [Guard(1), Guard(1)]
        g_out = Guard(1)
        for tt in range(NT):
            b = tt % 2
            t0 = tt * P

            @block.sync
            def _(sync):
                sync.wait_ge(s_ve.h, g_p6in[b].tok())
                s_p6l01[b].inc(sync.dma_start(out=ol_in[b][:, 0:1024], in_=d_ol[(2 * tt) * P:(2 * tt + 1) * P, :]), 16)
                s_p6l01[b].inc(sync.dma_start(out=ol_in[b][:, 1024:2048], in_=d_ol[(2 * tt + 1) * P:(2 * tt + 2) * P, :]), 16)
                s_p6l01[b].inc(sync.dma_start(out=qh_in[b][:, :], in_=d_qh[:, tt * D:(tt + 1) * D]), 16)
                s_p6l01[b].inc(sync.dma_start(out=gt_in[b][:, :], in_=d_gate[t0:t0 + P, :]), 16)
                s_p6l01[b].inc(sync.dma_start(out=x6[b][:, :], in_=ap_x[t0:t0 + P, :]), 16)
            p6l_tok = s_p6l01[b].n

            @block.tensor
            def _(t):
                t.wait_ge(s_p6l01[b].h, p6l_tok)
                t.wait_ge(s_ve.h, sib_tok)
                t.wait_ge(s_ve.h, g_corr.tok())
                ins = None
                for h in range(NH):
                    pps6 = ps_d2 if h < 8 else ps_e2
                    hb6 = (h % 8) * P
                    for cc_ in range(2):
                        ins = t.matmul(pps6[:, hb6 + cc_ * C:hb6 + cc_ * C + C],
                                       sinit_bf[:, h * P:(h + 1) * P],
                                       qh_in[b][:, h * P + cc_ * C:h * P + cc_ * C + C],
                                       start=True, stop=True)
                s_mm.inc(ins)
            corr_tok = s_mm.n

            @block.vector
            def _(v):
                v.wait_ge(s_mm.h, corr_tok)
                v.wait_ge(s_sc.h, g_of.tok())
                v.wait_ge(s_mm.h, g_ob.tok())
                olv = ol_in[b][:, :].rearrange("p (c h q) -> p h c q", c=2, h=NH)
                v.tensor_tensor(
                    out=of_sb[:, 0:1024].rearrange("p (h c q) -> p h c q", h=8, c=2),
                    in0=ps_d2[:, :].rearrange("p (h c q) -> p h c q", h=8, c=2),
                    in1=olv[:, 0:8],
                    op=ALU.add)
                v.tensor_tensor(
                    out=of_sb[:, 1024:2048].rearrange("p (h c q) -> p h c q", h=8, c=2),
                    in0=ps_e2[:, :].rearrange("p (h c q) -> p h c q", h=8, c=2),
                    in1=olv[:, 8:16],
                    op=ALU.add)
                s_ve.inc(v.tensor_tensor(
                    out=ob_sb[:, :].rearrange("p (h t) -> p h t", h=NH),
                    in0=of_sb[:, :].rearrange("p (h t) -> p h t", h=NH),
                    in1=gw_fm[:, :].rearrange("p (h o) -> p h o", o=1).broadcast_to([P, NH, P]),
                    op=ALU.mult))
            of_tok = s_ve.n
            g_corr.push(of_tok)

            @block.scalar
            def _(sc):
                sc.wait_ge(s_ve.h, of_tok)
                s_sc.inc(sc.activation(out=o2_sb[:, :], in_=of_sb[:, :], func=AF.Square))
            o2_tok = s_sc.n
            g_of.push(o2_tok)

            @block.tensor
            def _(t):
                t.wait_ge(s_sc.h, o2_tok)
                t.wait_ge(s_ve.h, g_ssq.tok())
                ins = None
                for h in range(NH):
                    ins = t.matmul(ps_c[:, 0:1], o2_sb[:, h * P:(h + 1) * P], ones1[:, 0:1],
                                   start=(h == 0), stop=(h == NH - 1))
                s_mm.inc(ins)
            ssq_tok = s_mm.n

            @block.scalar
            def _(sc):
                sc.wait_ge(s_mm.h, ssq_tok)
                s_sc.inc(sc.activation(out=ss_sb[:, 0:1], in_=ps_c[:, 0:1], func=AF.Sqrt,
                                       scale=float(1.0 / D), bias=np.float32(LN_EPS)))
            sstd_tok = s_sc.n

            @block.vector
            def _(v):
                v.wait_ge(s_sc.h, sstd_tok)
                v.reciprocal(out=rr_sb[:, 0:1], in_=ss_sb[:, 0:1]).then_inc(s_ve.h, 1)
                s_ve.n += 1
                v.wait_ge(s_ve.h, s_ve.n)
                s_ve.inc(v.tensor_copy(out=st1[:, 0:1], in_=rr_sb[:, 0:1]))
            rr_tok = s_ve.n
            g_ssq.push(rr_tok)

            last_wo_mm = 0
            for do in range(4):
                o0 = do * 512
                gdo = do % 2
                ppy = [ps_a, ps_b][gdo]

                @block.tensor
                def _(t):
                    t.wait_ge(s_ve.h, of_tok)
                    t.wait_ge(s_ve.h, wo_tok)
                    t.wait_ge(s_ve.h, g_ps6[gdo].tok())
                    ins = None
                    for h in range(NH):
                        ins = t.matmul(ppy[:, :], ob_sb[:, h * P:(h + 1) * P],
                                       wo_sb[:, h * D + o0:h * D + o0 + 512],
                                       start=(h == 0), stop=(h == NH - 1))
                    s_mm.inc(ins)
                smm_tok = s_mm.n
                last_wo_mm = smm_tok

                @block.vector
                def _(v):
                    v.wait_ge(s_mm.h, smm_tok)
                    v.wait_ge(s_ve.h, rr_tok)
                    if do == 0:
                        v.wait_ge(s_y.h, g_out.tok())
                    v.tensor_scalar(out=sv_sb[:, :], in0=ppy[:, :], scalar1=rr_sb[:, 0:1],
                                    scalar2=None, op0=ALU.mult)
                    v.tensor_tensor(out=sg_sb[:, :], in0=sv_sb[:, :],
                                    in1=gt_in[b][:, o0:o0 + 512], op=ALU.mult)
                    s_ve.inc(v.tensor_tensor(out=out_sb[:, o0:o0 + 512], in0=sg_sb[:, :],
                                             in1=x6[b][:, o0:o0 + 512], op=ALU.add))
                g_ps6[gdo].push(s_ve.n)
            odone_tok = s_ve.n
            g_ob.push(last_wo_mm)
            g_p6in[b].push(odone_tok)

            @block.sync
            def _(sync):
                sync.wait_ge(s_ve.h, odone_tok)
                s_y.inc(sync.dma_start(out=ap_y[t0:t0 + P, :], in_=out_sb[:, :]), 16)
            g_out.push(s_y.n)
            if tt % 4 == 3:
                block.flush()

        if KDBG:
            srcs = dict(ar_out=ar_out, d_sph=d_sph, d_d=d_d, d_v=d_v, d_q=d_q,
                        d_xqf=d_xqf, d_qh=d_qh, d_ol=d_ol, d_hq=d_hq, d_gate=d_gate,
                        pub_out=pub_out, wag1_out=wag1_out, d_scl=d_scl)
            srcs["d_stats"] = d_stats
            srcs["ar_in"] = ar_in
            srcs["d_warb"] = d_warb
            srcs["d_wsl"] = d_wsl
            dbg.update(dbg2)

            @block.sync
            def _(sync):
                sync.wait_ge(s_st.h, s_st.n)
                sync.wait_ge(s_ve.h, s_ve.n)
                sync.wait_ge(s_sc.h, s_sc.n)
                for nm, dst in dbg.items():
                    s_st.inc(sync.dma_start(out=dst[:, :], in_=srcs[nm][:, :]), 16)

        @block.sync
        def _(sync):
            sync.wait_ge(s_st.h, s_st.n)
            sync.wait_ge(s_y.h, s_y.n)
        block.flush()
        esE.close()

    return nc


state = {}
wq3_toks = []


def state_get_wq3(idx):
    return wq3_toks[idx - 1] if idx - 1 >= 0 and idx - 1 < len(wq3_toks) else 0


_NC_CACHE = [None]


def kernel(L0_attn_hidden_states, L0_attn_Wq, L0_attn_Wk, L0_attn_Wi, L0_attn_Wo,
           L0_attn_g_weight, L0_ffn_w1, L0_ffn_w2, L0_ffn_w3):
    x = np.ascontiguousarray(np.asarray(L0_attn_hidden_states, np.float32).reshape(16384, D))
    wq = np.ascontiguousarray(np.asarray(L0_attn_Wq, np.float32))
    wk = np.ascontiguousarray(np.asarray(L0_attn_Wk, np.float32))
    wi = np.ascontiguousarray(np.asarray(L0_attn_Wi, np.float32))
    wo = np.ascontiguousarray(np.asarray(L0_attn_Wo, np.float32))
    w1t = np.ascontiguousarray(np.asarray(L0_ffn_w1, np.float32).T)
    w3t = np.ascontiguousarray(np.asarray(L0_ffn_w3, np.float32).T)
    w2t = np.ascontiguousarray(np.asarray(L0_ffn_w2, np.float32).T)
    gwv = np.ascontiguousarray(np.asarray(L0_attn_g_weight, np.float32).reshape(1, D))

    tri_i = np.zeros((P, P), np.float32)
    tri_x = np.zeros((P, P), np.float32)
    io, jo = np.meshgrid(np.arange(C), np.arange(C), indexing="ij")
    for blk in range(2):
        s0 = blk * C
        tri_i[s0:s0 + C, s0:s0 + C] = (io <= jo)
        tri_x[s0:s0 + C, s0:s0 + C] = (io > jo)
    maskp = np.zeros((P, 1024), np.float32)
    for pp in range(P):
        kt = pp % C
        maskp[pp, :] = np.tile((kt <= np.arange(C)).astype(np.float32), NH)
    identm = np.eye(P, dtype=np.float32)
    cselh = np.zeros((P, 2), np.float32)
    cselh[0:C, 0] = 1
    cselh[C:P, 1] = 1
    onesh = np.ones((P, 1), np.float32)

    bfc = lambda a: a.astype(ml_dtypes.bfloat16)

    if _NC_CACHE[0] is None:
        state.clear()
        del wq3_toks[:]
        _NC_CACHE[0] = build_program()
    nc = _NC_CACHE[0]

    in_maps = []
    for c in range(NCORE):
        cmask = np.zeros((1, 8), np.float32)
        cmask[0, :c] = 1.0
        in_maps.append({
            "x": x[c * T:(c + 1) * T],
            "wq": wq, "wk": wk, "wi": wi, "wo": wo,
            "gwv": gwv, "cmask": cmask,
            "ctri_i": bfc(tri_i), "ctri_x": bfc(tri_x), "cident": bfc(identm),
            "cmaskp": maskp, "ccsel": bfc(cselh), "cones": bfc(onesh),
            "w1own": np.ascontiguousarray(w1t[c * (D // NCORE):(c + 1) * (D // NCORE)]),
            "w3own": np.ascontiguousarray(w3t[c * (D // NCORE):(c + 1) * (D // NCORE)]),
            "w2own": np.ascontiguousarray(w2t[c * W2ROWS:(c + 1) * W2ROWS]),
        })
    res = run_bass_kernel_spmd(nc, in_maps, core_ids=list(range(NCORE)))
    y = np.concatenate([np.asarray(res.results[c]["y"]) for c in range(NCORE)], axis=0)
    import os as _os
    if _os.environ.get("KERN_DEBUG") == "1":
        kernel.debug = [{k: np.asarray(v) for k, v in res.results[c].items()} for c in range(NCORE)]
    return y.reshape(1, 16384, D)
